# version 06
# baseline (speedup 1.0000x reference)
import contextlib
import numpy as np

import concourse.bass as bass
import concourse.mybir as mybir
from concourse.bass_utils import run_bass_kernel_spmd

F32 = mybir.dt.float32
BF16 = mybir.dt.bfloat16
AF = mybir.ActivationFunctionType
ALU = mybir.AluOpType
AX = mybir.AxisListType.X

TOK = 4096
EXT = 4224
NT128 = 33
SEQ = 8192
DM = 1024
FF = 2048
NE = 8
PAIRS = [[0, 1], [2, 3], [4, 5], [6, 7]]

TAPS = ()
SKIP_PRECAST = False
DELAY_GP = False


class Sem:
    def __init__(self, stack, nc, name):
        self.h = stack.enter_context(nc.semaphore(name))
        self.n = 0


class Q:

    def __init__(self, name):
        self.name = name
        self.ops = []

    def add(self, f):
        self.ops.append(f)

    def wait(self, sem, val):
        if val > 0:
            self.ops.append(lambda e: e.wait_ge(sem.h, val))

    def dma(self, sem, out, in_):
        sem.n += 16
        self.ops.append(lambda e: e.dma_start(out=out, in_=in_).then_inc(sem.h, 16))
        return sem.n

    def op(self, sem, f):
        sem.n += 1
        self.ops.append(lambda e: f(e).then_inc(sem.h, 1))
        return sem.n

    def opw(self, sem, f):
        m = self.op(sem, f)
        self.wait(sem, m)
        return m


def ap3(base_ap, offset_elems, dims):
    return bass.AP(base_ap.tensor, base_ap.offset + offset_elems, dims)


def build(taps=()):
    nc = bass.Bass("TRN2", target_bir_lowering=False, debug=False)
    stack = contextlib.ExitStack()

    def sbuf(name, shape, dtype=F32):
        return stack.enter_context(nc.sbuf_tensor(name, shape, dtype))

    ph_stacks = {}

    def psbuf(ph, name, shape, dtype=F32):
        st = ph_stacks.setdefault(ph, contextlib.ExitStack())
        return st.enter_context(nc.sbuf_tensor(name, shape, dtype))

    def close_phase(ph):
        if ph in ph_stacks:
            ph_stacks.pop(ph).close()

    pb = [stack.enter_context(nc.psum_tensor("pb%d" % i, [128, 512], F32)) for i in range(4)]
    psA = stack.enter_context(nc.psum_tensor("psA", [128, 1024], F32))
    psB = stack.enter_context(nc.psum_tensor("psB", [128, 1024], F32))
    pb = pb + [psA[:, 0:512], psA[:, 512:1024], psB[:, 0:512], psB[:, 512:1024]]

    def din(name, shape, dtype=F32):
        return nc.dram_tensor(name, shape, dtype, kind="ExternalInput").ap()

    def dint(name, shape, dtype=F32):
        return nc.dram_tensor(name, shape, dtype).ap()

    x_ext = din("x_ext", [EXT, DM])
    w_inT = din("w_inT", [DM, DM])
    kcat = din("kcat", [DM, 3])
    w_xprojT = din("w_xprojT", [512, 80])
    w_dtT_my = din("w_dtT_my", [64, 256])
    b_dt_my = din("b_dt_my", [256])
    b_dt_arr = din("b_dt_arr", [128, 2])
    A_arr = din("A_arr", [128, 16])
    D_arr = din("D_arr", [128, 2])
    woutT_y = din("woutT_y", [256, DM])
    woutT_z = din("woutT_z", [512, DM])
    wg = din("wg", [DM, 8])
    ws1 = din("ws1", [DM, FF])
    ws3 = din("ws3", [DM, FF])
    ws2 = din("ws2", [FF, DM])
    we1 = din("we1", [NE, DM, FF])
    we3 = din("we3", [NE, DM, FF])
    we2 = din("we2", [NE, FF, DM])
    wq_f = din("wq_f", [DM, DM])
    wk_f = din("wk_f", [DM, DM])
    RT = din("RT", [64, 8])
    bqhT = din("bqhT", [64, 16])
    bkhT = din("bkhT", [64, 16])
    wqT_f = din("wqT_f", [DM, DM])
    wkT_f = din("wkT_f", [DM, DM])
    bqh_my = din("bqh_my", [64, 8])
    bkh_my = din("bkh_my", [64, 8])
    wq_my = din("wq_my", [DM, 512])
    wk_my = din("wk_my", [DM, 512])
    wv_my = din("wv_my", [DM, 512])
    bq_my = din("bq_my", [512])
    bk_my = din("bk_my", [512])
    bv_my = din("bv_my", [512])
    wo_my = din("wo_my", [512, DM])
    bo = din("bo", [DM])
    w_ffnT = din("w_ffnT", [DM, DM])
    flagv = din("flagv", [128, 2])
    ident_f = din("ident_f", [128, 128])
    ident_b = din("ident_b", [128, 128], BF16)

    out_my = nc.dram_tensor("out_my", [TOK, DM], F32, kind="ExternalOutput").ap()

    XN0 = dint("XN0", [DM, EXT])
    ZB = dint("ZB", [512, TOK], BF16)
    a2a_in = dint("a2a_in", [1024, TOK], BF16)
    agxb_out = dint("agxb_out", [2048, TOK], BF16)
    agp_in = dint("agp_in", [TOK, 80])
    agp_out = dint("agp_out", [SEQ, 80])
    BCD = dint("BCD", [16, SEQ])
    prt_in = dint("prt_in", [2, TOK, DM])
    prt_out = dint("prt_out", [TOK, DM])
    X1 = dint("X1", [TOK, DM])
    agx_in = dint("agx_in", [1280, TOK], BF16)
    agx_out = dint("agx_out", [2560, TOK], BF16)
    art_in = dint("art_in", [2, TOK, DM])
    art_out = dint("art_out", [TOK, DM])

    tap_shapes = {
        "xn0": ([DM, EXT], F32), "zb": ([512, TOK], BF16),
        "a2a_in": ([1024, TOK], BF16), "proj": ([TOK, 80], F32),
        "a2a_out": ([2048, TOK], BF16), "agp_out": ([SEQ, 80], F32),
        "yT": ([2 * 128, SEQ], BF16), "delta": ([2 * 128, SEQ], F32),
        "bc": ([16, SEQ], F32),
        "prt_out": ([TOK, DM], F32), "x1": ([TOK, DM], F32),
        "agx_out": ([2560, TOK], BF16), "art_out": ([TOK, DM], F32),
        "wp": ([8, TOK], F32), "glog": ([128, 32 * 8], F32),
        "xntm": ([NT128, 128, DM], F32),
        "xraw": ([NT128, 128, DM], F32),
        "lnstats": ([NT128, 128, 16], F32),
    }
    tap_out = {}
    for t in taps:
        sh, dt = tap_shapes[t]
        tap_out[t] = nc.dram_tensor("tap_" + t, sh, dt, kind="ExternalOutput").ap()

    identf = sbuf("identf", [128, 128], F32)
    identb = sbuf("identb", [128, 128], BF16)
    kcol = sbuf("kcol", [128, 8 * 3], F32)
    wpTok = sbuf("wpTok", [128, 32 * 8], F32)
    yTs = psbuf("py", "yTs", [128, 2 * SEQ], BF16)
    dtrawT = psbuf("pdt", "dtrawT", [64, SEQ], BF16)
    w_inTb = psbuf("pw", "w_inTb", [128, 8 * DM], BF16)
    wxpTb = psbuf("pw", "wxpTb", [128, 4 * 80], BF16)

    SY = Q("sync")
    SC = Q("scalar")
    VE = Q("vector")
    TE = Q("tensor")
    GP = Q("gpsimd")

    def mksem(name):
        return Sem(stack, nc, name)

    def mksems(name, k):
        return [Sem(stack, nc, "%s%d" % (name, j)) for j in range(k)]

    sT = mksem("sT")
    sV = mksem("sV")
    sS = mksem("sS")
    sG = mksem("sG")

    ALLQ = []

    def fence_all(marks):
        for q_ in ALLQ:
            for se, v in marks:
                q_.wait(se, v)

    ALLQ.extend([SY, SC, VE, TE, GP])
    _delay_sem = [None, 0]
    sConst = mksem("sConst")
    SY.dma(sConst, identf[:], ident_f[:])
    SY.dma(sConst, identb[:], ident_b[:])
    SY.dma(sConst, ap3(kcol[:], 0, [[24, 128], [3, 8], [1, 3]]),
           ap3(kcat, 0, [[3, 128], [384, 8], [1, 3]]))
    p0_const = sConst.n

    sW0 = mksem("sW0")
    if DELAY_GP:
        GP.add(lambda e: e.wait_ge(_delay_sem[0].h, _delay_sem[1]))
    for ec in range(8):
        GP.dma(sW0, w_inTb[:, ec * DM:(ec + 1) * DM], w_inT[ec * 128:(ec + 1) * 128, :])
    for c in range(4):
        GP.dma(sW0, wxpTb[:, c * 80:(c + 1) * 80], w_xprojT[c * 128:(c + 1) * 128, :])
    p0_small = sW0.n
    sPC = mksem("sPC")
    moe_cast_done = sPC.n

    xt = psbuf("p1a", "xt", [128, 2 * DM], F32)
    stg = psbuf("p1a", "stg", [128, 2 * DM], F32)
    xnt = psbuf("p1a", "xnt", [128, 2 * DM], F32)
    st_ = psbuf("p1a", "st_", [128, 16], F32)

    sTapB = mksem("sTapB")
    sXL = mksems("sXL", 2)
    sXS = mksems("sXS", 2)
    ln_store = []
    nrm_mark = []
    trl_mark = []
    ev_hist = []
    for i in range(NT128):
        bb = (i % 2) * DM
        sl = i % 2
        xti = xt[:, bb:bb + DM]
        if i >= 2:
            SY.wait(sV, nrm_mark[i - 2])
        ld = SY.dma(sXL[sl], xti, x_ext[i * 128:(i + 1) * 128, :])

        VE.wait(sXL[sl], ld)
        if "xraw" in tap_out:
            GP.wait(sXL[sl], ld)
            xrd = GP.dma(sTapB, tap_out["xraw"][i], xti)
            SY.wait(sTapB, xrd)
        if i >= 2:
            VE.wait(sT, trl_mark[i - 2])
        VE.add(lambda e, a=st_[:, 0:1], b_=xti: e.reduce_sum(a, b_, axis=AX))
        VE.opw(sV, lambda e, a=stg[:, bb:bb + DM], b_=xti: e.tensor_tensor(a, b_, b_, ALU.mult))
        VE.opw(sV, lambda e, a=st_[:, 1:2], b_=stg[:, bb:bb + DM]: e.reduce_sum(a, b_, axis=AX))
        VE.add(lambda e: e.tensor_scalar(st_[:, 2:3], st_[:, 0:1], 1.0 / DM, None, ALU.mult))
        VE.opw(sV, lambda e: e.tensor_scalar(st_[:, 3:4], st_[:, 1:2], 1.0 / DM, None, ALU.mult))
        VE.opw(sV, lambda e: e.tensor_tensor(st_[:, 4:5], st_[:, 2:3], st_[:, 2:3], ALU.mult))
        VE.opw(sV, lambda e: e.tensor_tensor(st_[:, 5:6], st_[:, 3:4], st_[:, 4:5], ALU.subtract))
        var_m = VE.op(sV, lambda e: e.tensor_scalar(st_[:, 5:6], st_[:, 5:6], 1e-5, None, ALU.add))
        SC.wait(sV, var_m)
        sq_m = SC.op(sS, lambda e: e.sqrt(st_[:, 6:7], st_[:, 5:6]))
        VE.wait(sS, sq_m)
        rc_m = VE.op(sV, lambda e: e.reciprocal(st_[:, 7:8], st_[:, 6:7]))
        VE.wait(sV, rc_m)
        nm = VE.op(sV, lambda e, o=stg[:, bb:bb + DM], x_=xti: e.tensor_scalar(
            o, x_, st_[:, 2:3], st_[:, 7:8], ALU.subtract, ALU.mult))
        nrm_mark.append(nm)
        if "lnstats" in tap_out:
            GP.wait(sV, nm)
            GP.dma(sTapB, tap_out["lnstats"][i], st_[:])
        if "xntm" in tap_out:
            GP.wait(sV, nm)
            GP.dma(sTapB, tap_out["xntm"][i], stg[:, bb:bb + DM])

        TE.wait(sV, nm)
        if i == 0:
            TE.wait(sConst, p0_const)
        if i >= 2:
            SC.wait(sXS[sl], ln_store[i - 2])
        tm = 0
        for c in range(8):
            bank = pb[c % 2]
            if len(ev_hist) >= 2:
                TE.wait(sS, ev_hist[-2])
            tm = TE.op(sT, lambda e, o=bank[:, 0:128], s=stg[:, bb + c * 128:bb + (c + 1) * 128]:
                       e.transpose(o, s, identf[:]))
            SC.wait(sT, tm)
            em = SC.op(sS, lambda e, o=xnt[:, bb + c * 128:bb + (c + 1) * 128], s=bank[:, 0:128]:
                       e.copy(o, s))
            ev_hist.append(em)
        trl_mark.append(tm)
        SC.wait(sS, ev_hist[-1])
        stm = SC.dma(sXS[sl],
                     ap3(XN0, i * 128, [[EXT, 128], [128 * EXT, 8], [1, 128]]),
                     ap3(xnt[:], bb, [[2 * DM, 128], [128, 8], [1, 128]]))
        ln_store.append(stm)
    _p1a_tot = (sXS[0].n, sXS[1].n)
    _delay_sem[0], _delay_sem[1] = sXS[0], _p1a_tot[0]

    def wait_p1a(q):
        q.wait(sXS[0], _p1a_tot[0])
        q.wait(sXS[1], _p1a_tot[1])

    sTap = mksem("sTap")
    if "xn0" in tap_out:
        wait_p1a(SY)
        SY.dma(sTap, tap_out["xn0"][:], XN0[:])

    fence_all([(sXS[0], _p1a_tot[0]), (sXS[1], _p1a_tot[1]),
               (sV, sV.n), (sS, sS.n), (sT, sT.n),
               (sXL[0], sXL[0].n), (sXL[1], sXL[1].n)])
    close_phase("p1a")

    xnb = psbuf("p1b", "xnb", [128, 2 * 8 * 512], BF16)
    hbuf = psbuf("p1b", "hbuf", [128, 8 * 516], F32)
    cvb = psbuf("p1b", "cvb", [128, 512], F32)
    sgb = psbuf("p1b", "sgb", [128, 512], F32)
    xbt2 = psbuf("p1b", "xbt2", [128, 2 * 512], BF16)

    sXNB = mksems("sXNB", 2)
    sBS = mksems("sBS", 2)
    p1b_tiles = [(i * 512, 512) for i in range(8)] + [(4096, 128)]

    TE.wait(sW0, p0_small)
    VE.wait(sConst, p0_const)
    ev_all = ev_hist[-1]
    TE.wait(sS, ev_all)
    bank_free = [ev_all] * 8
    conv_done = 0
    tail_mark = 0
    sb_store = [0, 0]
    ht_marks = {}
    p1b_store_marks = []
    for ti, (t0, W) in enumerate(p1b_tiles):
        sl = ti % 2
        xnbs = xnb[:, sl * 4096:(sl + 1) * 4096]
        if ti >= 2:
            GP.wait(sT, ht_marks[(ti - 2, 7)])
        need_tile = (t0 + W + 127) // 128 - 1
        GP.wait(sXS[need_tile % 2], ln_store[need_tile])
        if need_tile >= 1:
            GP.wait(sXS[(need_tile - 1) % 2], ln_store[need_tile - 1])
        ldx = GP.dma(sXNB[sl], ap3(xnb[:], sl * 4096, [[2 * 8 * 512, 128], [512, 8], [1, W]]),
                     ap3(XN0, t0, [[EXT, 128], [128 * EXT, 8], [1, W]]))
        TE.wait(sXNB[sl], ldx)
        for cs in range(8):
            TE.wait(sS, bank_free[cs])
            for ec in range(8):
                mm = TE.op(sT, lambda e, o=pb[cs][:, 0:W],
                           l=w_inTb[:, ec * DM + cs * 128:ec * DM + (cs + 1) * 128],
                           r=ap3(xnb[:], sl * 4096 + ec * 512, [[2 * 8 * 512, 128], [1, W]]),
                           st=(ec == 0), sp=(ec == 7):
                           e.matmul(o, l, r, start=st, stop=sp))
            ht_marks[(ti, cs)] = mm
        if ti >= 1:
            SC.wait(sV, conv_done)
        ev_cs = []
        for cs in range(8):
            SC.wait(sT, ht_marks[(ti, cs)])
            em = SC.op(sS, lambda e, o=hbuf[:, cs * 516 + 2:cs * 516 + 2 + W], s=pb[cs][:, 0:W]:
                       e.copy(o, s))
            ev_cs.append(em)
            bank_free[cs] = em
        if ti == 0:
            jlo, jhi = 1, 511
        elif W == 512:
            jlo, jhi = t0 - 1, t0 + 511
        else:
            jlo, jhi = 4095, 4097
        Wp = jhi - jlo
        st_marks_tile = []
        for cs in range(8):
            VE.wait(sS, ev_cs[cs])
            if ti > 0:
                VE.wait(sV, tail_mark)
            base = jlo - t0 + 2
            h0 = hbuf[:, cs * 516 + base - 1:cs * 516 + base - 1 + Wp]
            h1 = hbuf[:, cs * 516 + base:cs * 516 + base + Wp]
            h2 = hbuf[:, cs * 516 + base + 1:cs * 516 + base + 1 + Wp]
            VE.opw(sV, lambda e, o=cvb[:, 0:Wp], a=h0, k=kcol[:, cs * 3:cs * 3 + 1]:
                   e.tensor_scalar(o, a, k, None, ALU.mult))
            VE.opw(sV, lambda e, o=cvb[:, 0:Wp], a=h1, k=kcol[:, cs * 3 + 1:cs * 3 + 2]:
                   e.scalar_tensor_tensor(o, a, k, o, ALU.mult, ALU.add))
            cm = VE.opw(sV, lambda e, o=cvb[:, 0:Wp], a=h2, k=kcol[:, cs * 3 + 2:cs * 3 + 3]:
                        e.scalar_tensor_tensor(o, a, k, o, ALU.mult, ALU.add))
            SC.wait(sV, cm)
            sgm = SC.op(sS, lambda e, o=sgb[:, 0:Wp], s=cvb[:, 0:Wp]:
                        e.activation(o, s, AF.Sigmoid))
            VE.wait(sS, sgm)
            ssl = cs % 2
            VE.wait(sBS[ssl], sb_store[ssl])
            xm = VE.opw(sV, lambda e, o=xbt2[:, ssl * 512:ssl * 512 + Wp],
                        a=cvb[:, 0:Wp], b_=sgb[:, 0:Wp]: e.tensor_tensor(o, a, b_, ALU.mult))
            SC.wait(sV, xm)
            dst = a2a_in[cs * 128:(cs + 1) * 128, jlo - 1:jhi - 1]
            sm = SC.dma(sBS[ssl], dst, xbt2[:, ssl * 512:ssl * 512 + Wp])
            sb_store[ssl] = sm
            st_marks_tile.append(sm)
        if ti < len(p1b_tiles) - 1:
            for cs in range(7):
                VE.add(lambda e, o=hbuf[:, cs * 516:cs * 516 + 2],
                       s=hbuf[:, cs * 516 + W:cs * 516 + W + 2]: e.tensor_copy(o, s))
            tail_mark = VE.op(sV, lambda e, o=hbuf[:, 7 * 516:7 * 516 + 2],
                              s=hbuf[:, 7 * 516 + W:7 * 516 + W + 2]: e.tensor_copy(o, s))
        conv_done = sV.n
        p1b_store_marks.append(list(st_marks_tile))
    p1b_bs_tot = (sBS[0].n, sBS[1].n)

    if "a2a_in" in tap_out:
        SY.wait(sBS[0], p1b_bs_tot[0])
        SY.wait(sBS[1], p1b_bs_tot[1])
        SY.dma(sTap, tap_out["a2a_in"][:], a2a_in[:])
    if "zb" in tap_out:
        SY.wait(sBS[0], p1b_bs_tot[0])
        SY.wait(sBS[1], p1b_bs_tot[1])
        SY.dma(sTap, tap_out["zb"][:], a2a_in[512:1024, :])

    xbl = psbuf("p1c", "xbl", [128, 2 * 4 * 128], BF16)
    pjs = psbuf("p1c", "pjs", [128, 2 * 80], F32)
    sPJL = mksems("sPJL", 2)
    sPJS = mksems("sPJS", 2)
    SY.wait(sBS[0], p1b_bs_tot[0])
    SY.wait(sBS[1], p1b_bs_tot[1])
    TE.wait(sW0, p0_small)
    pj_ev = {2: bank_free[2], 3: bank_free[3]}
    pj_mm = [0, 0]
    pjs_marks = [0, 0]
    for t in range(32):
        sl = t % 2
        if t >= 2:
            SY.wait(sT, pj_mm[sl])
        ld = SY.dma(sPJL[sl],
                    ap3(xbl[:], sl * 512, [[2 * 512, 128], [128, 4], [1, 128]]),
                    ap3(a2a_in, t * 128, [[TOK, 128], [128 * TOK, 4], [1, 128]]))
        bank = 2 + (t % 2)
        TE.wait(sPJL[sl], ld)
        TE.wait(sS, pj_ev[bank])
        for cc in range(4):
            mm = TE.op(sT, lambda e, o=pb[bank][:, 0:80],
                       l=ap3(xbl[:], sl * 512 + cc * 128, [[2 * 512, 128], [1, 128]]),
                       r=wxpTb[:, cc * 80:(cc + 1) * 80],
                       st=(cc == 0), sp=(cc == 3): e.matmul(o, l, r, start=st, stop=sp))
        pj_mm[sl] = mm
        SC.wait(sT, mm)
        SC.wait(sPJS[sl], pjs_marks[sl])
        em = SC.op(sS, lambda e, o=pjs[:, sl * 80:(sl + 1) * 80], s=pb[bank][:, 0:80]:
                   e.copy(o, s))
        pj_ev[bank] = em
        SC.wait(sS, em)
        sm = SC.dma(sPJS[sl], agp_in[t * 128:(t + 1) * 128, :], pjs[:, sl * 80:(sl + 1) * 80])
        pjs_marks[sl] = sm
    p1c_tot = (sPJS[0].n, sPJS[1].n)
    if "proj" in tap_out:
        SY.wait(sPJS[0], p1c_tot[0])
        SY.wait(sPJS[1], p1c_tot[1])
        SY.dma(sTap, tap_out["proj"][:], agp_in[:])

    sCCp = mksem("sCCp")
    sCCx = mksem("sCCx")
    GP.wait(sPJS[0], p1c_tot[0])
    GP.wait(sPJS[1], p1c_tot[1])
    GP.op(sCCp, lambda e: e.collective_compute(
        "AllGather", mybir.AluOpType.bypass, replica_groups=PAIRS,
        ins=[agp_in[:]], outs=[agp_out[:]]))
    GP.wait(sBS[0], p1b_bs_tot[0])
    GP.wait(sBS[1], p1b_bs_tot[1])
    GP.op(sCCx, lambda e: e.collective_compute(
        "AllGather", mybir.AluOpType.bypass, replica_groups=PAIRS,
        ins=[a2a_in[:]], outs=[agxb_out[:]]))
    cc_proj, cc_xb = sCCp.n, sCCx.n

    if "agp_out" in tap_out:
        SY.wait(sCCp, cc_proj)
        SY.dma(sTap, tap_out["agp_out"][:], agp_out[:])
    if "a2a_out" in tap_out:
        SY.wait(sCCx, cc_xb)
        SY.dma(sTap, tap_out["a2a_out"][:], agxb_out[:])

    sAGL = mksems("sAGL", 2)
    sBCS = mksems("sBCS", 2)
    agl = psbuf("p2", "agl", [128, 2 * 80], F32)
    bcStage = psbuf("p2", "bcStage", [16, 2 * 128], F32)
    SY.wait(sCCp, cc_proj)
    ag_banks = {4: bank_free[4], 5: bank_free[5]}
    agl_marks = [0, 0]
    bcs_marks = [0, 0]
    bc_last = 0
    for t in range(64):
        sl = t % 2
        if t >= 2:
            SY.wait(sT, agl_marks[sl])
        ld = SY.dma(sAGL[sl], agl[:, sl * 80:(sl + 1) * 80], agp_out[t * 128:(t + 1) * 128, :])
        bank = 4 + (t % 2)
        TE.wait(sAGL[sl], ld)
        TE.wait(sS, ag_banks[bank])
        tm = TE.op(sT, lambda e, o=pb[bank][0:80, 0:128], s=agl[:, sl * 80:(sl + 1) * 80]:
                   e.transpose(o, s, identf[:]))
        agl_marks[sl] = tm
        SC.wait(sT, tm)
        SC.add(lambda e, o=ap3(dtrawT[:], t * 128, [[SEQ, 64], [1, 128]]),
               s=pb[bank][0:64, 0:128]: e.copy(o, s))
        SC.wait(sBCS[sl], bcs_marks[sl])
        em = SC.op(sS, lambda e, o=bcStage[:, sl * 128:(sl + 1) * 128],
                   s=pb[bank][64:80, 0:128]: e.activation(o, s, AF.Tanh))
        ag_banks[bank] = em
        SC.wait(sS, em)
        bm = SC.dma(sBCS[sl], ap3(BCD, t * 128, [[SEQ, 16], [1, 128]]),
                    ap3(bcStage[:], sl * 128, [[2 * 128, 16], [1, 128]]))
        bcs_marks[sl] = bm
        bc_last = em
    p2_done = bc_last
    p2_bcs_tot = (sBCS[0].n, sBCS[1].n)
    if "bc" in tap_out:
        SY.wait(sBCS[0], p2_bcs_tot[0])
        SY.wait(sBCS[1], p2_bcs_tot[1])
        SY.dma(sTap, tap_out["bc"][:], BCD[:])

    fence_all([(sV, sV.n), (sS, sS.n), (sT, sT.n),
               (sBS[0], sBS[0].n), (sBS[1], sBS[1].n),
               (sPJL[0], sPJL[0].n), (sPJL[1], sPJL[1].n),
               (sPJS[0], sPJS[0].n), (sPJS[1], sPJS[1].n),
               (sAGL[0], sAGL[0].n), (sAGL[1], sAGL[1].n),
               (sXNB[0], sXNB[0].n), (sXNB[1], sXNB[1].n),
               (sBCS[0], sBCS[0].n), (sBCS[1], sBCS[1].n),
               (sCCp, sCCp.n), (sCCx, sCCx.n)])
    close_phase("p2")
    close_phase("p1c")
    close_phase("p1b")
    close_phase("pw")

    sW3 = mksem("sW3")
    carry = psbuf("pdt", "carry", [128, 16], F32)
    flg = psbuf("pdt", "flg", [128, 2], F32)
    w_dtTb = psbuf("pdt", "w_dtTb", [64, 256], BF16)
    abuf = psbuf("pdt", "abuf", [128, 16 * 4], F32)
    dcol = psbuf("pdt", "dcol", [128, 2], F32)
    bdt2 = psbuf("pdt", "bdt2", [128, 2], F32)
    GP.dma(sW3, w_dtTb[:], w_dtT_my[:])
    GP.dma(sW3, abuf[:, 0:16], A_arr[:])
    GP.dma(sW3, dcol[:], D_arr[:])
    GP.dma(sW3, bdt2[:], b_dt_arr[:])
    GP.dma(sW3, flg[:], flagv[:])
    w3_tot = sW3.n
    SC.wait(sW3, w3_tot)
    e1 = SC.op(sS, lambda e: e.activation(abuf[:, 16:32], abuf[:, 0:16], AF.Exp))
    VE.wait(sS, e1)
    v1 = VE.opw(sV, lambda e: e.tensor_scalar(abuf[:, 16:32], abuf[:, 16:32], 1.0, None, ALU.add))
    SC.wait(sV, v1)
    e2 = SC.op(sS, lambda e: e.activation(abuf[:, 16:32], abuf[:, 16:32], AF.Ln))
    VE.wait(sS, e2)
    VE.opw(sV, lambda e: e.tensor_scalar(abuf[:, 16:32], abuf[:, 16:32], 1e-4, None, ALU.add))
    aprep = VE.opw(sV, lambda e: e.tensor_scalar(abuf[:, 32:48], abuf[:, 16:32], -1.0, None, ALU.mult))

    TC = 2048
    delta_sb = psbuf("p3", "delta_sb", [128, TC], F32)
    xbsel = psbuf("p3", "xbsel", [128, TC], F32)
    decb = psbuf("p3", "decb", [128, TC], F32)
    ub = psbuf("p3", "ub", [128, TC], F32)
    drv = psbuf("p3", "drv", [128, TC], F32)
    sout = psbuf("p3", "sout", [128, TC], F32)
    yacc = psbuf("p3", "yacc", [128, TC], F32)
    tmp3 = psbuf("p3", "tmp3", [128, TC], F32)
    brep = psbuf("p3", "brep", [128, 2 * TC], F32)
    crep = psbuf("p3", "crep", [128, 2 * TC], F32)
    xbc = psbuf("p3", "xbc", [128, 2 * TC], BF16)
    sBR = mksems("sBR", 2)
    sXBL = mksems("sXBL", 2)
    TE.wait(sS, p2_done)
    TE.wait(sW3, sW3.n)
    VE.wait(sW3, sW3.n)
    dl_banks = {6: bank_free[6], 7: bank_free[7]}
    brc_mark = [0, 0]
    sel_prev = [0]
    ych_marks = []
    for tc in range(4):
        th, lc = tc // 2, (tc % 2) * TC
        for ch in range(2):
            SY.wait(sV, sel_prev[0])
            xl = SY.dma(sXBL[ch], ap3(xbc[:], 0, [[2 * TC, 128], [TC, 2], [1, TC]]),
                        ap3(agxb_out, th * 1024 * TOK + ch * 128 * TOK + lc,
                            [[TOK, 128], [256 * TOK, 2], [1, TC]]))
            VE.wait(sXBL[ch], xl)
            VE.opw(sV, lambda e: e.tensor_scalar(xbsel[:], xbc[:, 0:TC], flg[:, 0:1], None, ALU.mult))
            selm = VE.opw(sV, lambda e: e.scalar_tensor_tensor(
                xbsel[:], xbc[:, TC:2 * TC], flg[:, 1:2], xbsel[:], ALU.mult, ALU.add))
            pass
            for sub in range(4):
                bank = 6 + (sub % 2)
                TE.wait(sS, dl_banks[bank])
                dm = TE.op(sT, lambda e, o=pb[bank][:, 0:512],
                           l=w_dtTb[:, ch * 128:(ch + 1) * 128],
                           r=ap3(dtrawT[:], tc * TC + sub * 512, [[SEQ, 64], [1, 512]]):
                           e.matmul(o, l, r, start=True, stop=True))
                SC.wait(sT, dm)
                em = SC.op(sS, lambda e, o=delta_sb[:, sub * 512:(sub + 1) * 512],
                           s=pb[bank][:, 0:512], b_=bdt2[:, ch:ch + 1]:
                           e.activation(o, s, AF.Exp, bias=b_))
                dl_banks[bank] = em
                VE.wait(sS, em)
                a1 = VE.op(sV, lambda e, o=delta_sb[:, sub * 512:(sub + 1) * 512]:
                           e.tensor_scalar(o, o, 1.0, None, ALU.add))
                SC.wait(sV, a1)
                lm = SC.op(sS, lambda e, o=delta_sb[:, sub * 512:(sub + 1) * 512]:
                           e.activation(o, o, AF.Ln))
                VE.wait(sS, lm)
                VE.opw(sV, lambda e, o=delta_sb[:, sub * 512:(sub + 1) * 512]:
                       e.tensor_scalar(o, o, 1e-4, 1.0, ALU.max, ALU.min))
            for n in range(8):
                nb = n % 2
                SY.wait(sV, brc_mark[nb])
                br = SY.dma(sBR[nb], ap3(brep[:], nb * TC, [[2 * TC, 128], [0, 1], [1, TC]]),
                            ap3(BCD, n * SEQ + tc * TC, [[0, 128], [0, 1], [1, TC]]))
                cr = SY.dma(sBR[nb], ap3(crep[:], nb * TC, [[2 * TC, 128], [0, 1], [1, TC]]),
                            ap3(BCD, (8 + n) * SEQ + tc * TC, [[0, 128], [0, 1], [1, TC]]))
                VE.wait(sBR[nb], cr)
                acol = abuf[:, 32 + ch * 8 + n:32 + ch * 8 + n + 1]
                dts = VE.opw(sV, lambda e, a=acol: e.tensor_scalar(decb[:], delta_sb[:], a, None, ALU.mult))
                SC.wait(sV, dts)
                ex = SC.op(sS, lambda e: e.activation(decb[:], decb[:], AF.Exp))
                VE.wait(sS, ex)
                VE.opw(sV, lambda e: e.tensor_scalar(decb[:], decb[:], 1e-4, 1.0, ALU.max, ALU.min))
                VE.opw(sV, lambda e, b_=brep[:, nb * TC:(nb + 1) * TC]:
                       e.tensor_tensor(ub[:], xbsel[:], b_, ALU.mult))
                VE.opw(sV, lambda e: e.tensor_tensor(drv[:], decb[:], ub[:], ALU.mult))
                VE.opw(sV, lambda e: e.tensor_tensor(drv[:], ub[:], drv[:], ALU.subtract))
                init = 0.0 if tc == 0 else carry[:, ch * 8 + n:ch * 8 + n + 1]
                VE.opw(sV, lambda e, i_=init: e.tensor_tensor_scan(
                    out=sout[:], data0=decb[:], data1=drv[:], initial=i_,
                    op0=ALU.mult, op1=ALU.add))
                VE.opw(sV, lambda e, o=carry[:, ch * 8 + n:ch * 8 + n + 1]:
                       e.tensor_copy(o, sout[:, TC - 1:TC]))
                cr_ = crep[:, nb * TC:(nb + 1) * TC]
                if n == 0:
                    VE.opw(sV, lambda e, c_=cr_: e.tensor_tensor(yacc[:], sout[:], c_, ALU.mult))
                else:
                    VE.opw(sV, lambda e, c_=cr_: e.tensor_tensor(tmp3[:], sout[:], c_, ALU.mult))
                    VE.opw(sV, lambda e: e.tensor_tensor(yacc[:], yacc[:], tmp3[:], ALU.add))
                brc_mark[nb] = sV.n
            sel_prev[0] = sV.n
            ym = VE.opw(sV, lambda e, d_=dcol[:, ch:ch + 1],
                        o=ap3(yTs[:], ch * SEQ + tc * TC, [[2 * SEQ, 128], [1, TC]]):
                        e.scalar_tensor_tensor(o, xbsel[:], d_, yacc[:], ALU.mult, ALU.add))
            ych_marks.append(ym)
            sel_prev[0] = ym
    p3_done = sV.n
    if "yT" in tap_out:
        VE.wait(sV, p3_done)
        SY.wait(sV, p3_done)
        SY.dma(sTap, ap3(tap_out["yT"], 0, [[SEQ, 128], [1, SEQ]]),
               ap3(yTs[:], 0, [[2 * SEQ, 128], [1, SEQ]]))
        SY.dma(sTap, ap3(tap_out["yT"], 128 * SEQ, [[SEQ, 128], [1, SEQ]]),
               ap3(yTs[:], SEQ, [[2 * SEQ, 128], [1, SEQ]]))

    fence_all([(sV, sV.n), (sS, sS.n), (sT, sT.n),
               (sBR[0], sBR[0].n), (sBR[1], sBR[1].n),
               (sXBL[0], sXBL[0].n), (sXBL[1], sXBL[1].n)])
    close_phase("p3")
    close_phase("pdt")

    sW4 = mksem("sW4")
    wyb = psbuf("p4", "wyb", [128, 2 * DM], BF16)
    wzf = psbuf("p4", "wzf", [128, 4 * DM], F32)
    wzb = psbuf("p4", "wzb", [128, 4 * DM], BF16)
    for c2 in range(2):
        GP.dma(sW4, wyb[:, c2 * DM:(c2 + 1) * DM], woutT_y[c2 * 128:(c2 + 1) * 128, :])
    for c4 in range(4):
        GP.dma(sW4, wzf[:, c4 * DM:(c4 + 1) * DM], woutT_z[c4 * 128:(c4 + 1) * 128, :])
    w4_tot = sW4.n
    VE.wait(sW4, w4_tot)
    wz_m = VE.opw(sV, lambda e: e.tensor_scalar(wzb[:], wzf[:], 0.5, None, ALU.mult))
    zbl = psbuf("p4", "zbl", [128, 2 * 4 * 128], BF16)
    pst = psbuf("p4", "pst", [128, 2 * DM], F32)
    sZL = mksems("sZL", 2)
    sPS = mksems("sPS", 2)
    TE.wait(sV, p3_done)
    TE.wait(sV, wz_m)
    SY.wait(sCCx, cc_xb)
    bank_guard = {0: bank_free[0], 1: bank_free[1], 2: pj_ev[2], 3: pj_ev[3]}
    zl_mark = [0, 0]
    ps_mark = [0, 0]
    prt_marks = []
    for t in range(64):
        th, lt = t // 32, (t % 32) * 128
        sl = t % 2
        if t >= 2:
            SY.wait(sT, zl_mark[sl])
        zl = SY.dma(sZL[sl], ap3(zbl[:], sl * 512, [[1024, 128], [128, 4], [1, 128]]),
                    ap3(agxb_out, th * 1024 * TOK + 512 * TOK + lt,
                        [[TOK, 128], [128 * TOK, 4], [1, 128]]))
        TE.wait(sZL[sl], zl)
        SC.wait(sPS[sl], ps_mark[sl])
        mmz = 0
        for eh in range(2):
            bank = sl * 2 + eh
            TE.wait(sS, bank_guard[bank])
            for c2 in range(2):
                TE.op(sT, lambda e, o=pb[bank][:, 0:512],
                      l=ap3(yTs[:], c2 * SEQ + t * 128, [[2 * SEQ, 128], [1, 128]]),
                      r=wyb[:, c2 * DM + eh * 512:c2 * DM + eh * 512 + 512],
                      st=(c2 == 0), sp=False: e.matmul(o, l, r, start=st, stop=sp))
            for c4 in range(4):
                mmz = TE.op(sT, lambda e, o=pb[bank][:, 0:512],
                            l=ap3(zbl[:], sl * 512 + c4 * 128, [[1024, 128], [1, 128]]),
                            r=wzb[:, c4 * DM + eh * 512:c4 * DM + eh * 512 + 512],
                            st=False, sp=(c4 == 3): e.matmul(o, l, r, start=st, stop=sp))
            zl_mark[sl] = mmz
            SC.wait(sT, mmz)
            em = SC.op(sS, lambda e, o=pst[:, sl * DM + eh * 512:sl * DM + (eh + 1) * 512],
                       s=pb[bank][:, 0:512]: e.copy(o, s))
            bank_guard[bank] = em
        SC.wait(sS, bank_guard[sl * 2 + 1])
        pm = SC.dma(sPS[sl], prt_in[th, lt:lt + 128, :], pst[:, sl * DM:(sl + 1) * DM])
        ps_mark[sl] = pm
        prt_marks.append(pm)
    p4_ps_tot = (sPS[0].n, sPS[1].n)
    sCCr = mksem("sCCr")
    GP.wait(sPS[0], p4_ps_tot[0])
    GP.wait(sPS[1], p4_ps_tot[1])
    GP.op(sCCr, lambda e: e.collective_compute(
        "ReduceScatter", mybir.AluOpType.add, replica_groups=PAIRS,
        ins=[prt_in[:]], outs=[prt_out[:]]))
    cc_prt = sCCr.n
    if "prt_out" in tap_out:
        SY.wait(sCCr, cc_prt)
        SY.dma(sTap, tap_out["prt_out"][:], prt_out[:])

    fence_all([(sV, sV.n), (sS, sS.n), (sT, sT.n),
               (sZL[0], sZL[0].n), (sZL[1], sZL[1].n),
               (sPS[0], sPS[0].n), (sPS[1], sPS[1].n), (sW4, sW4.n)])
    close_phase("p4")
    close_phase("py")

    MOE = dint("MOE", [TOK, DM])
    sW5 = mksem("sW5")
    wgs = psbuf("p5a", "wgs", [128, 8 * 8], F32)
    xnf = psbuf("p5a", "xnf", [128, 2 * 8 * 128], F32)
    gt = psbuf("p5a", "gt", [128, 16], F32)
    gsc = psbuf("p5a", "gsc", [128, 8], F32)
    GP.dma(sW5, ap3(wgs[:], 0, [[64, 128], [8, 8], [1, 8]]),
           ap3(wg, 0, [[8, 128], [128 * 8, 8], [1, 8]]))
    w5_tot = sW5.n
    sGL = mksems("sGL", 2)
    TE.wait(sW5, w5_tot)
    g_banks = {6: dl_banks[6], 7: dl_banks[7]}
    gl_mm = [0, 0]
    for t in range(32):
        sl = t % 2
        if t >= 2:
            SY.wait(sT, gl_mm[sl])
        gl = SY.dma(sGL[sl], ap3(xnf[:], sl * 1024, [[2 * 1024, 128], [128, 8], [1, 128]]),
                    ap3(XN0, 1 + t * 128, [[EXT, 128], [128 * EXT, 8], [1, 128]]))
        bank = 6 + sl
        TE.wait(sGL[sl], gl)
        TE.wait(sS, g_banks[bank])
        for ec in range(8):
            mm = TE.op(sT, lambda e, o=pb[bank][:, 0:8],
                       l=ap3(xnf[:], sl * 1024 + ec * 128, [[2 * 1024, 128], [1, 128]]),
                       r=wgs[:, ec * 8:(ec + 1) * 8],
                       st=(ec == 0), sp=(ec == 7): e.matmul(o, l, r, start=st, stop=sp))
        gl_mm[sl] = mm
        SC.wait(sT, mm)
        if t >= 1:
            SC.wait(sV, p5a_v_last)
        gm = SC.op(sS, lambda e, o=gt[:, 0:8], s=pb[bank][:, 0:8]:
                   e.activation(o, s, AF.Sigmoid))
        g_banks[bank] = gm
        VE.wait(sS, gm)
        VE.opw(sV, lambda e: e.reduce_max(gsc[:, 0:1], gt[:, 0:8], axis=AX))
        VE.opw(sV, lambda e: e.tensor_scalar(gt[:, 8:16], gt[:, 0:8], gsc[:, 0:1], None, ALU.is_equal))
        VE.opw(sV, lambda e: e.scalar_tensor_tensor(gsc[:, 4:5], gt[:, 8:9], 0.0, gt[:, 8:9], ALU.mult, ALU.add)) if False else None
        VE.opw(sV, lambda e: e.scalar_tensor_tensor(gt[:, 0:8], gt[:, 8:16], -1e30, gt[:, 0:8], ALU.mult, ALU.add))
        VE.opw(sV, lambda e: e.reduce_max(gsc[:, 1:2], gt[:, 0:8], axis=AX))
        VE.opw(sV, lambda e: e.tensor_scalar(gt[:, 0:8], gt[:, 0:8], gsc[:, 1:2], None, ALU.is_equal))
        VE.opw(sV, lambda e: e.tensor_tensor(gsc[:, 2:3], gsc[:, 0:1], gsc[:, 1:2], ALU.add))
        VE.opw(sV, lambda e: e.tensor_scalar(gsc[:, 2:3], gsc[:, 2:3], 3.0, None, ALU.mult))
        VE.opw(sV, lambda e: e.reciprocal(gsc[:, 3:4], gsc[:, 2:3]))
        VE.opw(sV, lambda e: e.tensor_scalar(gt[:, 8:16], gt[:, 8:16], gsc[:, 0:1], None, ALU.mult))
        VE.opw(sV, lambda e: e.scalar_tensor_tensor(gt[:, 8:16], gt[:, 0:8], gsc[:, 1:2], gt[:, 8:16], ALU.mult, ALU.add))
        p5a_v_last = VE.opw(sV, lambda e, o=wpTok[:, t * 8:(t + 1) * 8]:
               e.tensor_scalar(o, gt[:, 8:16], gsc[:, 3:4], None, ALU.mult))
    p5a_done = sV.n
    if "wp" in tap_out:
        SY.wait(sV, p5a_done)
        SY.dma(sTap, tap_out["wp"][:], wpTok[:])
    fence_all([(sV, sV.n), (sS, sS.n), (sT, sT.n),
               (sGL[0], sGL[0].n), (sGL[1], sGL[1].n), (sW5, w5_tot)])
    close_phase("p5a")

    xnb5 = psbuf("p5b", "xnb5", [128, 8 * 2048], BF16)
    h5 = psbuf("p5b", "h5", [128, 16 * 2048], BF16)
    acc5 = psbuf("p5b", "acc5", [128, 16 * DM], F32)
    w1sl = psbuf("p5b", "w1sl", [128, 2 * 1024], BF16)
    w3sl = psbuf("p5b", "w3sl", [128, 2 * 1024], BF16)
    wd5 = psbuf("p5b", "wd5", [128, 16 * 1024], BF16)
    s5 = psbuf("p5b", "s5", [128, 2 * 512], F32)
    sX5 = mksem("sX5")
    sWU = mksems("sWU", 2)
    sWDg = mksems("sWDg", 2)
    sMS = mksems("sMS", 2)
    exps = [(ws1[:], ws3[:], ws2[:], None)] + [(we1[ei], we3[ei], we2[ei], ei) for ei in range(NE)]
    moe_store = [0, 0]
    hm_prev = [0]
    silu_prev = [0]
    hm_sl = [0, 0]
    p5_bank0 = {0: bank_guard[0], 1: bank_guard[1], 2: bank_guard[2], 3: bank_guard[3],
                4: ag_banks[4], 5: ag_banks[5], 6: g_banks[6], 7: g_banks[7]}
    last_up_mm = 0
    wd_load = [0, 0]
    wd_free = [0, 0]
    for sp in range(2):
        if sp == 1:
            GP.wait(sT, last_up_mm)
        VE.wait(sMS[0], moe_store[0])
        VE.wait(sMS[1], moe_store[1])
        x5m = GP.dma(sX5, ap3(xnb5[:], 0, [[8 * 2048, 128], [2048, 8], [1, 2048]]),
                     ap3(XN0, 1 + sp * 2048, [[EXT, 128], [128 * EXT, 8], [1, 2048]]))
        TE.wait(sX5, x5m)
        if sp == 0:
            TE.wait(sV, p5a_done)
        wu_mark = [0, 0]
        for xi, (W1, W3, W2, ei) in enumerate(exps):
            ep = xi % 2
            up_last = 0
            for fs in range(16):
                slw = fs % 2
                GP.wait(sT, wu_mark[slw])
                l1 = GP.dma(sWU[slw], ap3(w1sl[:], slw * 1024, [[2 * 1024, 128], [128, 8], [1, 128]]),
                            ap3(W1, fs * 128, [[FF, 128], [128 * FF, 8], [1, 128]]))
                l3 = GP.dma(sWU[slw], ap3(w3sl[:], slw * 1024, [[2 * 1024, 128], [128, 8], [1, 128]]),
                            ap3(W3, fs * 128, [[FF, 128], [128 * FF, 8], [1, 128]]))
                if fs == 0:
                    GP.wait(sT, wd_free[0])
                w2m = GP.dma(sWDg[0], ap3(wd5[:], fs * 1024, [[16 * 1024, 128], [1, 1024]]),
                             ap3(W2, fs * 128 * DM, [[DM, 128], [1, 1024]]))
                TE.wait(sWU[slw], l3)
                mmu = 0
                if sp == 0 and xi == 0 and fs == 0:
                    for bank in range(8):
                        TE.wait(sS, p5_bank0[bank])
                    silu_tt = [0, 0, 0, 0]
                    hm_tt = [0, 0, 0, 0]
                hm = 0
                sm = 0
                for tt in range(4):
                    ts5 = tt % 2
                    TE.wait(sS, silu_tt[tt])
                    mm1 = 0
                    for ec in range(8):
                        mm1 = TE.op(sT, lambda e, o=pb[tt][:, 0:512],
                                    l=ap3(w1sl[:], slw * 1024 + ec * 128, [[2 * 1024, 128], [1, 128]]),
                                    r=ap3(xnb5[:], ec * 2048 + tt * 512, [[8 * 2048, 128], [1, 512]]),
                                    st=(ec == 0), sp=(ec == 7):
                                    e.matmul(o, l, r, start=st, stop=sp))
                    TE.wait(sV, hm_tt[tt])
                    mmu = 0
                    for ec in range(8):
                        mmu = TE.op(sT, lambda e, o=pb[4 + tt][:, 0:512],
                                    l=ap3(w3sl[:], slw * 1024 + ec * 128, [[2 * 1024, 128], [1, 128]]),
                                    r=ap3(xnb5[:], ec * 2048 + tt * 512, [[8 * 2048, 128], [1, 512]]),
                                    st=(ec == 0), sp=(ec == 7):
                                    e.matmul(o, l, r, start=st, stop=sp))
                    SC.wait(sT, mm1)
                    if hm_sl[ts5]:
                        SC.wait(sV, hm_sl[ts5])
                    sm = SC.op(sS, lambda e, o=s5[:, ts5 * 512:(ts5 + 1) * 512], s=pb[tt][:, 0:512]:
                               e.activation(o, s, AF.Silu))
                    silu_tt[tt] = sm
                    VE.wait(sS, sm)
                    VE.wait(sT, mmu)
                    hm = VE.op(sV, lambda e, o=h5[:, fs * 2048 + tt * 512:fs * 2048 + (tt + 1) * 512],
                               a=s5[:, ts5 * 512:(ts5 + 1) * 512], b_=pb[4 + tt][:, 0:512]:
                               e.tensor_tensor(o, a, b_, ALU.mult))
                    hm_tt[tt] = hm
                    hm_sl[ts5] = hm
                wu_mark[slw] = mmu
                last_up_mm = mmu
                up_last = mmu
                hm_prev[0] = hm
                silu_prev[0] = sm
            TE.wait(sV, hm_prev[0])
            TE.wait(sWDg[0], w2m)
            dn_last_v = 0
            mmd = 0
            for tt2 in range(16):
                b0, b1 = (0, 1) if tt2 % 2 == 0 else (2, 3)
                if tt2 >= 2:
                    TE.wait(sV, dn_marks[(tt2 - 2) % 2])
                for fc in range(16):
                    for eh in range(2):
                        mmd = TE.op(sT, lambda e, o=pb[(b0, b1)[eh]][:, 0:512],
                                    l=ap3(h5[:], fc * 2048 + tt2 * 128, [[16 * 2048, 128], [1, 128]]),
                                    r=ap3(wd5[:], fc * 1024 + eh * 512, [[16 * 1024, 128], [1, 512]]),
                                    st=(fc == 0), sp=(fc == 15):
                                    e.matmul(o, l, r, start=st, stop=sp))
                gt_i = (sp * 16 + tt2)
                VE.wait(sT, mmd)
                vm = 0
                for eh in range(2):
                    bank = (b0, b1)[eh]
                    dst5 = acc5[:, tt2 * DM + eh * 512:tt2 * DM + (eh + 1) * 512]
                    if ei is None:
                        vm = VE.op(sV, lambda e, o=dst5, s=pb[bank][:, 0:512]:
                                   e.tensor_scalar(o, s, 1.0 / 3.0, None, ALU.mult))
                    else:
                        vm = VE.op(sV, lambda e, o=dst5, s=pb[bank][:, 0:512],
                                   w_=wpTok[:, gt_i * 8 + ei:gt_i * 8 + ei + 1]:
                                   e.scalar_tensor_tensor(o, s, w_, o, ALU.mult, ALU.add))
                VE.wait(sV, vm)
                if tt2 == 0 and (sp == 0 and xi == 0):
                    dn_marks = [0, 0]
                dn_marks[tt2 % 2] = vm
            wd_free[0] = mmd
            TE.wait(sV, max(dn_marks))
        for tt2 in range(16):
            sls = tt2 % 2
            SC.wait(sV, dn_marks[tt2 % 2] if False else sV.n)
            SC.wait(sMS[sls], moe_store[sls])
            ms = SC.dma(sMS[sls], MOE[sp * 2048 + tt2 * 128:sp * 2048 + (tt2 + 1) * 128, :],
                        acc5[:, tt2 * DM:(tt2 + 1) * DM])
            moe_store[sls] = ms
    p5b_ms_tot = (sMS[0].n, sMS[1].n)

    fence_all([(sV, sV.n), (sS, sS.n), (sT, sT.n), (sX5, sX5.n),
               (sWU[0], sWU[0].n), (sWU[1], sWU[1].n), (sWDg[0], sWDg[0].n),
               (sMS[0], sMS[0].n), (sMS[1], sMS[1].n),
               (sGL[0], sGL[0].n), (sGL[1], sGL[1].n), (sW5, sW5.n),
               (sCCr, cc_prt)])
    close_phase("p5b")

    wrq = psbuf("p5c2", "wrq", [128, 8 * 128], F32)
    wrk = psbuf("p5c2", "wrk", [128, 8 * 128], F32)
    bqRb = psbuf("p5c2", "bqRb", [128, 128], F32)
    bkRb = psbuf("p5c2", "bkRb", [128, 128], F32)
    rts = psbuf("p5c", "rts", [64, 8], F32)
    wqTrow = psbuf("p5c", "wqTrow", [64, 2 * DM], F32)
    bqh_s = psbuf("p5c", "bqh_s", [64, 64], F32)
    brow = psbuf("p5c", "brow", [1, 256], F32)
    x0t5 = psbuf("p5c", "x0t5", [128, 2 * DM], F32)
    prtt = psbuf("p5c", "prtt", [128, 2 * DM], F32)
    moet = psbuf("p5c", "moet", [128, 2 * DM], F32)
    x1t5 = psbuf("p5c", "x1t5", [128, 2 * DM], F32)
    st5 = psbuf("p5c", "st5", [128, 16], F32)
    xnT5 = psbuf("p5c", "xnT5", [128, 2 * DM], F32)
    qr5 = psbuf("p5c", "qr5", [128, 128], F32)
    m45 = psbuf("p5c", "m45", [128, 64], F32)
    m1x = psbuf("p5c", "m1x", [128, 128], F32)
    oh5 = psbuf("p5c", "oh5", [128, 128], F32)
    ohst = psbuf("p5c", "ohst", [128, 2 * 128], BF16)
    sWR = mksem("sWR")
    sWQR = mksems("sWQR", 2)
    sL5 = mksems("sL5", 2)
    sS5 = mksems("sS5", 2)
    sA5 = mksems("sA5", 2)
    sO5 = mksems("sO5", 2)
    sBB = mksem("sBB")

    GP.dma(sWR, rts[:], RT[:])
    GP.dma(sWR, bqh_s[:, 0:16], bqhT[:])
    GP.dma(sWR, bqh_s[:, 16:32], bkhT[:])
    wr_tot = sWR.n
    wql_mark = [0, 0]
    for side, (wrdst, wT) in enumerate(((wrq, wqT_f), (wrk, wkT_f))):
        for hh in range(16):
            slh = hh % 2
            SY.wait(sT, wql_mark[slh])
            lw = SY.dma(sWQR[slh], wqTrow[:, slh * DM:(slh + 1) * DM], wT[hh * 64:hh * 64 + 64, :])
            TE.wait(sWQR[slh], lw)
            TE.wait(sWR, wr_tot)
            for ec in range(8):
                mmw = TE.op(sT, lambda e, o=pb[2][0:128, 0:8],
                            l=ap3(wqTrow[:], slh * DM + ec * 128, [[2 * DM, 64], [1, 128]]),
                            r=rts[:], st=True, sp=True: e.matmul(o, l, r, start=st, stop=sp))
                wql_mark[slh] = mmw
                SC.wait(sT, mmw)
                wr_ev = SC.op(sS, lambda e, o=wrdst[:, ec * 128 + hh * 8:ec * 128 + hh * 8 + 8],
                              s=pb[2][0:128, 0:8]: e.copy(o, s))
                TE.wait(sS, wr_ev)
    TE.wait(sWR, wr_tot)
    for side in range(2):
        mmb = TE.op(sT, lambda e, o=pb[2][0:16, 0:8],
                    l=bqh_s[:, side * 16:(side + 1) * 16],
                    r=rts[:], st=True, sp=True: e.matmul(o, l, r, start=st, stop=sp))
        SC.wait(sT, mmb)
        ev = SC.op(sS, lambda e, o=bqh_s[0:16, 32 + side * 16:32 + side * 16 + 8],
                   s=pb[2][0:16, 0:8]: e.copy(o, s))
        TE.wait(sS, ev)
        GP.wait(sS, ev)
        m1_ = GP.dma(sBB, ap3(brow[:], side * 128, [[256, 1], [8, 16], [1, 8]]),
                     ap3(bqh_s[:], 32 + side * 16, [[64, 16], [1, 8]]))
        GP.wait(sBB, m1_)
        GP.dma(sBB, ap3((bqRb, bkRb)[side][:], 0, [[128, 128], [0, 1], [1, 128]]),
               ap3(brow[:], side * 128, [[256, 1], [0, 128], [1, 128]]))
    bb_tot = sBB.n
    VE.wait(sBB, bb_tot)

    x1s_marks = []
    ag_st = [0, 0]
    oh_st = [0, 0]
    tr_banks5 = {0: 0, 1: 0}
    qrk_bank = {3: 0}
    l5_marks = [0, 0]
    for t in range(32):
        sl = t % 2
        bb = sl * DM
        if t >= 2:
            SY.wait(sV, x1s_marks[t - 2])
        l1 = SY.dma(sL5[sl], x0t5[:, bb:bb + DM], x_ext[1 + t * 128:1 + (t + 1) * 128, :])
        l2 = SY.dma(sL5[sl], prtt[:, bb:bb + DM], prt_out[t * 128:(t + 1) * 128, :])
        l3 = SY.dma(sL5[sl], moet[:, bb:bb + DM], MOE[t * 128:(t + 1) * 128, :])
        VE.wait(sL5[sl], l3)
        VE.opw(sV, lambda e, o=x1t5[:, bb:bb + DM], a=x0t5[:, bb:bb + DM], b_=prtt[:, bb:bb + DM]:
               e.tensor_tensor(o, a, b_, ALU.add))
        xm1 = VE.opw(sV, lambda e, o=x1t5[:, bb:bb + DM], b_=moet[:, bb:bb + DM]:
                     e.tensor_tensor(o, o, b_, ALU.add))
        SC.wait(sV, xm1)
        sx1 = SC.dma(sS5[sl], X1[t * 128:(t + 1) * 128, :], x1t5[:, bb:bb + DM])
        VE.add(lambda e, a=st5[:, 0:1], b_=x1t5[:, bb:bb + DM]: e.reduce_sum(a, b_, axis=AX))
        VE.opw(sV, lambda e, a=moet[:, bb:bb + DM], b_=x1t5[:, bb:bb + DM]:
               e.tensor_tensor(a, b_, b_, ALU.mult))
        VE.opw(sV, lambda e, a=st5[:, 1:2], b_=moet[:, bb:bb + DM]: e.reduce_sum(a, b_, axis=AX))
        VE.add(lambda e: e.tensor_scalar(st5[:, 2:3], st5[:, 0:1], 1.0 / DM, None, ALU.mult))
        VE.opw(sV, lambda e: e.tensor_scalar(st5[:, 3:4], st5[:, 1:2], 1.0 / DM, None, ALU.mult))
        VE.opw(sV, lambda e: e.tensor_tensor(st5[:, 4:5], st5[:, 2:3], st5[:, 2:3], ALU.mult))
        VE.opw(sV, lambda e: e.tensor_tensor(st5[:, 5:6], st5[:, 3:4], st5[:, 4:5], ALU.subtract))
        var5 = VE.op(sV, lambda e: e.tensor_scalar(st5[:, 5:6], st5[:, 5:6], 1e-5, None, ALU.add))
        SC.wait(sV, var5)
        sq5 = SC.op(sS, lambda e: e.sqrt(st5[:, 6:7], st5[:, 5:6]))
        VE.wait(sS, sq5)
        rc5 = VE.opw(sV, lambda e: e.reciprocal(st5[:, 7:8], st5[:, 6:7]))
        VE.wait(sS5[sl], sx1)
        nm5 = VE.op(sV, lambda e, o=x1t5[:, bb:bb + DM], x_=x1t5[:, bb:bb + DM]:
                    e.tensor_scalar(o, x_, st5[:, 2:3], st5[:, 7:8], ALU.subtract, ALU.mult))
        x1s_marks.append(nm5)
        TE.wait(sV, nm5)
        for c in range(8):
            bank = pb[c % 2]
            if t > 0 or c >= 2:
                TE.wait(sS, tr_banks5[c % 2])
            tmk = TE.op(sT, lambda e, o=bank[:, 0:128], s=x1t5[:, bb + c * 128:bb + (c + 1) * 128]:
                        e.transpose(o, s, identf[:]))
            SC.wait(sT, tmk)
            if t >= 2 and c < 2:
                SC.wait(sA5[sl], ag_st[sl])
            evk = SC.op(sS, lambda e, o=xnT5[:, bb + c * 128:bb + (c + 1) * 128], s=bank[:, 0:128]:
                        e.copy(o, s))
            tr_banks5[c % 2] = evk
        GP.wait(sS, tr_banks5[(8 - 1) % 2])
        GP.wait(sS, tr_banks5[0 if (8 - 1) % 2 == 1 else 1])
        agm = GP.dma(sA5[sl], ap3(agx_in, t * 128, [[TOK, 128], [128 * TOK, 8], [1, 128]]),
                     ap3(xnT5[:], bb, [[2 * DM, 128], [128, 8], [1, 128]]))
        ag_st[sl] = agm
        TE.wait(sS, tr_banks5[0])
        TE.wait(sS, tr_banks5[1])
        for side, (wrs, brb, rowoff) in enumerate(((wrq, bqRb, 1024), (wrk, bkRb, 1152))):
            TE.wait(sV, qrk_bank[3])
            for ec in range(8):
                qmm = TE.op(sT, lambda e, o=pb[3][:, 0:128],
                            l=ap3(xnT5[:], bb + ec * 128, [[2 * DM, 128], [1, 128]]),
                            r=wrs[:, ec * 128:(ec + 1) * 128],
                            st=(ec == 0), sp=(ec == 7): e.matmul(o, l, r, start=st, stop=sp))
            VE.wait(sT, qmm)
            VE.opw(sV, lambda e, b_=brb[:]: e.tensor_tensor(qr5[:], pb[3][:, 0:128], b_, ALU.add))
            qrk_bank[3] = sV.n
            VE.opw(sV, lambda e: e.tensor_tensor(
                ap3(m45[:], 0, [[64, 128], [4, 16], [1, 4]]),
                ap3(qr5[:], 0, [[128, 128], [8, 16], [1, 4]]),
                ap3(qr5[:], 4, [[128, 128], [8, 16], [1, 4]]), ALU.max))
            VE.opw(sV, lambda e: e.tensor_tensor(
                ap3(m45[:], 32, [[64, 128], [2, 16], [1, 2]]),
                ap3(m45[:], 0, [[64, 128], [4, 16], [1, 2]]),
                ap3(m45[:], 2, [[64, 128], [4, 16], [1, 2]]), ALU.max))
            VE.opw(sV, lambda e: e.tensor_tensor(
                ap3(m45[:], 48, [[64, 128], [1, 16], [1, 1]]),
                ap3(m45[:], 32, [[64, 128], [2, 16], [1, 1]]),
                ap3(m45[:], 33, [[64, 128], [2, 16], [1, 1]]), ALU.max))
            VE.opw(sV, lambda e: e.tensor_copy(
                ap3(m1x[:], 0, [[128, 128], [8, 16], [1, 1]]),
                ap3(m45[:], 48, [[64, 128], [1, 16], [1, 1]])))
            VE.opw(sV, lambda e: e.tensor_copy(
                ap3(m1x[:], 1, [[128, 128], [8, 16], [1, 1]]),
                ap3(m1x[:], 0, [[128, 128], [8, 16], [1, 1]])))
            VE.opw(sV, lambda e: e.tensor_copy(
                ap3(m1x[:], 2, [[128, 128], [8, 16], [1, 2]]),
                ap3(m1x[:], 0, [[128, 128], [8, 16], [1, 2]])))
            VE.opw(sV, lambda e: e.tensor_copy(
                ap3(m1x[:], 4, [[128, 128], [8, 16], [1, 4]]),
                ap3(m1x[:], 0, [[128, 128], [8, 16], [1, 4]])))
            ohm = VE.opw(sV, lambda e: e.tensor_tensor(oh5[:], qr5[:], m1x[:], ALU.is_equal))
            TE.wait(sV, ohm)
            TE.wait(sS, tr_banks5[0])
            tmo = TE.op(sT, lambda e, o=pb[0][:, 0:128], s=oh5[:]: e.transpose(o, s, identf[:]))
            SC.wait(sT, tmo)
            SC.wait(sO5[side], oh_st[side])
            sc_ = 30000.0 if side == 0 else 1.0
            eco = SC.op(sS, lambda e, o=ohst[:, side * 128:side * 128 + 128], s=pb[0][:, 0:128],
                        k=sc_: e.activation(o, s, AF.Copy, scale=k))
            tr_banks5[0] = eco
            SC.wait(sS, eco)
            som = SC.dma(sO5[side], ap3(agx_in, rowoff * TOK + t * 128, [[TOK, 128], [1, 128]]),
                         ohst[:, side * 128:side * 128 + 128])
            oh_st[side] = som
    p5c_sx_tot = (sS5[0].n, sS5[1].n)
    p5c_ag_tot = (sA5[0].n, sA5[1].n)
    p5c_oh_tot = (sO5[0].n, sO5[1].n)
    if "x1" in tap_out:
        SY.wait(sS5[0], p5c_sx_tot[0])
        SY.wait(sS5[1], p5c_sx_tot[1])
        SY.dma(sTap, tap_out["x1"][:], X1[:])

    sCC6 = mksem("sCC6")
    GP.wait(sA5[0], p5c_ag_tot[0])
    GP.wait(sA5[1], p5c_ag_tot[1])
    GP.wait(sO5[0], p5c_oh_tot[0])
    GP.wait(sO5[1], p5c_oh_tot[1])
    GP.op(sCC6, lambda e: e.collective_compute(
        "AllGather", mybir.AluOpType.bypass, replica_groups=PAIRS,
        ins=[agx_in[:]], outs=[agx_out[:]]))
    cc_agx = sCC6.n
    if "agx_out" in tap_out:
        SY.wait(sCC6, cc_agx)
        SY.dma(sTap, tap_out["agx_out"][:], agx_out[:])

    fence_all([(sV, sV.n), (sS, sS.n), (sT, sT.n), (sCC6, cc_agx),
               (sL5[0], sL5[0].n), (sL5[1], sL5[1].n),
               (sS5[0], p5c_sx_tot[0]), (sS5[1], p5c_sx_tot[1]),
               (sWQR[0], sWQR[0].n), (sWQR[1], sWQR[1].n), (sBB, bb_tot)])
    close_phase("p5c")

    sW6 = mksem("sW6")
    wqb = psbuf("p6", "wqb", [128, 8 * 512], BF16)
    wkb = psbuf("p6", "wkb", [128, 8 * 512], BF16)
    wvb = psbuf("p6", "wvb", [128, 8 * 512], BF16)
    wob = psbuf("p6", "wob", [128, 4 * DM], BF16)
    bqh8 = psbuf("p6", "bqh8", [64, 32], F32)
    bvrow = psbuf("p6", "bvrow", [1, 512], F32)
    bvb = psbuf("p6", "bvb", [128, 512], F32)
    QT6 = psbuf("p6", "QT6", [128, SEQ], BF16)
    KT6 = psbuf("p6", "KT6", [128, SEQ], BF16)
    vp6 = psbuf("p6", "vp6", [128, 64 * 65], BF16)
    aot = psbuf("p6", "aot", [128, 4 * SEQ], BF16)
    xt6 = psbuf("p6", "xt6", [128, 2 * 8 * 512], BF16)
    PT6 = psbuf("p6", "PT6", [128, 2 * 1024], BF16)
    rd6 = psbuf("p6", "rd6", [1, 2 * 512], F32)
    cm30 = psbuf("p6", "cm30", [128, 1], F32)
    rdb6 = psbuf("p6", "rdb6", [64, 2 * 512], F32)
    art_st = psbuf("p6", "art_st", [128, 2 * DM], F32)
    for ec in range(8):
        GP.dma(sW6, wqb[:, ec * 512:(ec + 1) * 512], wq_my[ec * 128:(ec + 1) * 128, :])
        GP.dma(sW6, wkb[:, ec * 512:(ec + 1) * 512], wk_my[ec * 128:(ec + 1) * 128, :])
        GP.dma(sW6, wvb[:, ec * 512:(ec + 1) * 512], wv_my[ec * 128:(ec + 1) * 128, :])
    for c4 in range(4):
        GP.dma(sW6, wob[:, c4 * DM:(c4 + 1) * DM], wo_my[c4 * 128:(c4 + 1) * 128, :])
    GP.dma(sW6, bqh8[:, 0:8], bqh_my[:])
    GP.dma(sW6, bqh8[:, 8:16], bkh_my[:])
    bvl = GP.dma(sW6, bvrow[0:1, :], bv_my[None, :])
    GP.add(lambda e, v=bvl: e.wait_ge(sW6.h, v))
    GP.dma(sW6, ap3(bvb[:], 0, [[512, 128], [0, 1], [1, 512]]),
           ap3(bvrow[:], 0, [[512, 1], [0, 128], [1, 512]]))
    w6_tot = sW6.n
    VE.wait(sW6, w6_tot)
    bq8m = VE.opw(sV, lambda e: e.tensor_scalar(bqh8[0:64, 16:24], bqh8[0:64, 0:8], 0.125, None, ALU.mult))
    on_m = VE.opw(sV, lambda e: e.memset(ap3(vp6[:], 64, [[64 * 65, 128], [65, 64], [1, 1]]), 1.0))
    cm_m = VE.opw(sV, lambda e: e.memset(cm30[:], -30016.0))

    sA6 = mksems("sA6", 2)
    sOH6 = mksem("sOH6")
    sRB6 = mksems("sRB6", 2)
    sAR = mksems("sAR", 2)
    qk_banks = {0: 0, 1: 0, 2: 0, 3: 0}
    sc_banks = {66: 0, 67: 0}
    v_banks = {3: 0}
    pv_bank6 = {0: 0, 3: 0}
    rb_slot = [0, 0]
    an_slot = [0, 0]
    pt_con = [0, 0]
    a6_mm = [0, 0]
    TE.wait(sW6, w6_tot)
    TE.wait(sV, on_m)
    TE.wait(sV, cm_m)
    TE.wait(sV, bq8m)
    SC.wait(sV, cm_m)
    art_marks = [0, 0]
    aot_done = 0
    last_score_mm = 0
    for h in range(8):
        if h > 0:
            GP.wait(sT, last_score_mm)
        for th in range(2):
            GP.dma(sOH6, QT6[64:72, th * TOK:(th + 1) * TOK],
                   agx_out[th * 1280 + 1024 + h * 8:th * 1280 + 1024 + h * 8 + 8, :])
            GP.dma(sOH6, KT6[64:72, th * TOK:(th + 1) * TOK],
                   agx_out[th * 1280 + 1152 + h * 8:th * 1280 + 1152 + h * 8 + 8, :])
        oh_tot = sOH6.n
        for tq in range(16):
            sl6 = tq % 2
            if h > 0 or tq >= 2:
                SY.wait(sT, a6_mm[sl6])
            th = tq // 8
            la = SY.dma(sA6[sl6], ap3(xt6[:], sl6 * 4096, [[2 * 8 * 512, 128], [512, 8], [1, 512]]),
                        ap3(agx_out, th * 1280 * TOK + (tq % 8) * 512, [[TOK, 128], [128 * TOK, 8], [1, 512]]))
            TE.wait(sA6[sl6], la)
            bk_ = 1
            TE.wait(sV, qk_banks[bk_])
            for ec in range(8):
                mk = TE.op(sT, lambda e, o=pb[bk_][0:64, 0:512],
                           l=wkb[:, ec * 512 + h * 64:ec * 512 + h * 64 + 64],
                           r=ap3(xt6[:], sl6 * 4096 + ec * 512, [[2 * 8 * 512, 128], [1, 512]]),
                           st=(ec == 0), sp=(ec == 7): e.matmul(o, l, r, start=st, stop=sp))
            bq_ = 2
            TE.wait(sV, qk_banks[bq_])
            for ec in range(8):
                mq = TE.op(sT, lambda e, o=pb[bq_][0:64, 0:512],
                           l=wqb[:, ec * 512 + h * 64:ec * 512 + h * 64 + 64],
                           r=ap3(xt6[:], sl6 * 4096 + ec * 512, [[2 * 8 * 512, 128], [1, 512]]),
                           st=(ec == 0), sp=(ec == 7): e.matmul(o, l, r, start=st, stop=sp))
            a6_mm[sl6] = mq
            VE.wait(sT, mk)
            ek = VE.op(sV, lambda e, o=ap3(KT6[:], tq * 512, [[SEQ, 64], [1, 512]]),
                       s=pb[bk_][0:64, 0:512], b_=bqh8[0:64, 8 + h:9 + h]:
                       e.tensor_scalar(o, s, b_, None, ALU.add))
            qk_banks[bk_] = ek
            VE.wait(sT, mq)
            eq = VE.op(sV, lambda e, o=ap3(QT6[:], tq * 512, [[SEQ, 64], [1, 512]]),
                       s=pb[bq_][0:64, 0:512], b_=bqh8[0:64, 16 + h:17 + h]:
                       e.tensor_scalar(o, s, 0.125, b_, ALU.mult, ALU.add))
            qk_banks[bq_] = eq
            for ts4 in range(4):
                vb_ = 3
                TE.wait(sV, v_banks[vb_])
                TE.wait(sV, pv_bank6[3])
                for ec in range(8):
                    mv = TE.op(sT, lambda e, o=pb[vb_][:, 0:64],
                               l=ap3(xt6[:], sl6 * 4096 + ec * 512 + ts4 * 128, [[2 * 8 * 512, 128], [1, 128]]),
                               r=wvb[:, ec * 512 + h * 64:ec * 512 + h * 64 + 64],
                               st=(ec == 0), sp=(ec == 7): e.matmul(o, l, r, start=st, stop=sp))
                a6_mm[sl6] = mv
                VE.wait(sT, mv)
                ev = VE.opw(sV, lambda e, o=ap3(vp6[:], (tq * 4 + ts4) * 65, [[64 * 65, 128], [1, 64]]),
                            s=pb[vb_][:, 0:64], b_=bvb[:, h * 64:h * 64 + 64]:
                            e.tensor_tensor(o, s, b_, ALU.add))
                v_banks[vb_] = ev
        TE.wait(sOH6, oh_tot)
        TE.wait(sV, v_banks[3])
        for qc in range(16):
            pvi = 0 if qc % 2 == 0 else 3
            pvb = pb[pvi]
            if h == 0 and qc == 0:
                TE.wait(sS, g_banks[6])
                TE.wait(sS, g_banks[7])
            TE.wait(sV, pv_bank6[pvi])
            exp_m = [0, 0]
            for j in range(32):
                pj = j % 2
                ps_ = psA if pj == 0 else psB
                if j == 0:
                    TE.wait(sS, sc_banks[66])
                else:
                    TE.wait(sS, exp_m[(j - 1) % 2])
                    for half in range(2):
                        kt = 2 * (j - 1) + half
                        mpv = TE.op(sT, lambda e, o=pvb[0:65, 0:512],
                                    l=ap3(vp6[:], kt * 65, [[64 * 65, 128], [1, 65]]),
                                    r=PT6[:, ((j - 1) % 2) * 1024 + half * 512:
                                            ((j - 1) % 2) * 1024 + (half + 1) * 512],
                                    st=(kt == 0), sp=False: e.matmul(o, l, r, start=st, stop=sp))
                    pt_con[(j - 1) % 2] = mpv
                for half in range(2):
                    kt = 2 * j + half
                    ms = TE.op(sT, lambda e, o=ps_[:, half * 512:(half + 1) * 512],
                               l=ap3(KT6[:], kt * 128, [[SEQ, 72], [1, 128]]),
                               r=ap3(QT6[:], qc * 512, [[SEQ, 72], [1, 512]]),
                               st=True, sp=True: e.matmul(o, l, r, start=st, stop=sp))
                last_score_mm = ms
                SC.wait(sT, ms)
                exp_m[pj] = SC.op(sS, lambda e, o=PT6[:, pj * 1024:(pj + 1) * 1024],
                                  s=ps_[:, 0:1024]:
                                  e.activation(o, s, AF.Exp, bias=cm30[:, 0:1]))
                sc_banks[66 + pj] = exp_m[pj]
            TE.wait(sS, exp_m[31 % 2])
            for half in range(2):
                kt = 62 + half
                mpv = TE.op(sT, lambda e, o=pvb[0:65, 0:512],
                            l=ap3(vp6[:], kt * 65, [[64 * 65, 128], [1, 65]]),
                            r=PT6[:, (31 % 2) * 1024 + half * 512:(31 % 2) * 1024 + (half + 1) * 512],
                            st=False, sp=(kt == 63): e.matmul(o, l, r, start=st, stop=sp))
            pt_con[31 % 2] = mpv
            slr = qc % 2
            VE.wait(sT, mpv)
            VE.wait(sRB6[slr], rb_slot[slr])
            rc6 = VE.opw(sV, lambda e, o=rd6[0:1, slr * 512:slr * 512 + 512],
                         s=pvb[64:65, 0:512]: e.reciprocal(o, s))
            GP.wait(sV, rc6)
            GP.wait(sV, an_slot[slr])
            rb = GP.dma(sRB6[slr], ap3(rdb6[:], slr * 512, [[1024, 64], [0, 1], [1, 512]]),
                        ap3(rd6[:], slr * 512, [[1024, 1], [0, 64], [1, 512]]))
            rb_slot[slr] = rb
            VE.wait(sRB6[slr], rb)
            po = (h % 2) * 64
            an = VE.opw(sV, lambda e,
                        o=aot[po:po + 64, (h // 2) * SEQ + qc * 512:(h // 2) * SEQ + (qc + 1) * 512],
                        s=pvb[0:64, 0:512], r_=rdb6[:, slr * 512:slr * 512 + 512]:
                        e.tensor_tensor(o, s, r_, ALU.mult))
            an_slot[slr] = an
            pv_bank6[pvi] = an
        aot_done = sV.n

    TE.wait(sV, aot_done)
    TE.wait(sV, qk_banks[0])
    TE.wait(sV, qk_banks[1])
    TE.wait(sV, qk_banks[2])
    TE.wait(sV, qk_banks[3])
    ar_banks = {0: 0, 1: 0, 2: 0, 3: 0}
    ar_ps = [0, 0]
    for t in range(64):
        th, lt = t // 32, (t % 32) * 128
        sl = t % 2
        SC.wait(sAR[sl], ar_ps[sl])
        mwo = 0
        for eh in range(2):
            bank = sl * 2 + eh
            TE.wait(sS, ar_banks[bank])
            for cp in range(4):
                mwo = TE.op(sT, lambda e, o=pb[bank][:, 0:512],
                            l=ap3(aot[:], cp * SEQ + t * 128, [[4 * SEQ, 128], [1, 128]]),
                            r=wob[:, cp * DM + eh * 512:cp * DM + eh * 512 + 512],
                            st=(cp == 0), sp=(cp == 3): e.matmul(o, l, r, start=st, stop=sp))
            SC.wait(sT, mwo)
            ea = SC.op(sS, lambda e, o=art_st[:, sl * DM + eh * 512:sl * DM + (eh + 1) * 512],
                       s=pb[bank][:, 0:512]: e.copy(o, s))
            ar_banks[bank] = ea
        SC.wait(sS, ar_banks[sl * 2 + 1])
        pa = SC.dma(sAR[sl], art_in[th, lt:lt + 128, :], art_st[:, sl * DM:(sl + 1) * DM])
        ar_ps[sl] = pa
    p6_ar_tot = (sAR[0].n, sAR[1].n)
    sCC7 = mksem("sCC7")
    GP.wait(sAR[0], p6_ar_tot[0])
    GP.wait(sAR[1], p6_ar_tot[1])
    GP.op(sCC7, lambda e: e.collective_compute(
        "ReduceScatter", mybir.AluOpType.add, replica_groups=PAIRS,
        ins=[art_in[:]], outs=[art_out[:]]))
    cc_art = sCC7.n
    if "art_out" in tap_out:
        SY.wait(sCC7, cc_art)
        SY.dma(sTap, tap_out["art_out"][:], art_out[:])

    fence_all([(sV, sV.n), (sS, sS.n), (sT, sT.n), (sW6, w6_tot),
               (sA6[0], sA6[0].n), (sA6[1], sA6[1].n), (sOH6, sOH6.n),
               (sRB6[0], sRB6[0].n), (sRB6[1], sRB6[1].n),
               (sAR[0], p6_ar_tot[0]), (sAR[1], p6_ar_tot[1])])
    close_phase("p6")

    sL7 = mksems("sL7", 6)
    sO7 = mksems("sO7", 2)
    sWF = mksems("sWF", 2)
    sW7 = mksem("sW7")
    sgn_src = psbuf("p7", "sgn_src", [128, 2 * DM], F32)
    sgnb = psbuf("p7", "sgnb", [128, 8 * DM], BF16)
    borow = psbuf("p7", "borow", [1, DM], F32)
    bob = psbuf("p7", "bob", [128, DM], F32)
    x17 = psbuf("p7", "x17", [128, 2 * DM], F32)
    art7 = psbuf("p7", "art7", [128, 2 * DM], F32)
    xnl7 = psbuf("p7", "xnl7", [128, 2 * 8 * 128], BF16)
    x27 = psbuf("p7", "x27", [128, 2 * DM], F32)
    bol = GP.dma(sW7, borow[0:1, :], bo[None, :])
    GP.add(lambda e, v=bol: e.wait_ge(sW7.h, v))
    GP.dma(sW7, ap3(bob[:], 0, [[DM, 128], [0, 1], [1, DM]]),
           ap3(borow[:], 0, [[DM, 1], [0, 128], [1, DM]]))
    w7_tot = sW7.n
    sgf_marks = [0, 0]
    for ec in range(8):
        slf = ec % 2
        if ec >= 2:
            SY.wait(sS, sgf_marks[slf])
        lf = SY.dma(sWF[slf], sgn_src[:, slf * DM:(slf + 1) * DM], w_ffnT[ec * 128:(ec + 1) * 128, :])
        SC.wait(sWF[slf], lf)
        sg_m = SC.op(sS, lambda e, o=sgnb[:, ec * DM:(ec + 1) * DM], s=sgn_src[:, slf * DM:(slf + 1) * DM]:
                     e.sign(o, s))
        sgf_marks[slf] = sg_m
    sgn_done = sS.n
    VE.wait(sW7, w7_tot)
    TE.wait(sS, sS.n)
    TE.wait(sV, sV.n)
    f7_banks = {4: 0, 5: 0, 6: 0, 7: 0}
    SY.wait(sCC7, cc_art)
    o7_marks = [0, 0]
    l7_con = [0, 0]
    for t in range(32):
        sl = t % 2
        bb = sl * DM
        if t >= 2:
            SY.wait(sV, l7_con[sl])
        la = SY.dma(sL7[sl], x17[:, bb:bb + DM], X1[t * 128:(t + 1) * 128, :])
        lb = SY.dma(sL7[2 + sl], art7[:, bb:bb + DM], art_out[t * 128:(t + 1) * 128, :])
        lc = SY.dma(sL7[4 + sl], ap3(xnl7[:], sl * 1024, [[2 * 1024, 128], [128, 8], [1, 128]]),
                    ap3(agx_in, t * 128, [[TOK, 128], [128 * TOK, 8], [1, 128]]))
        TE.wait(sL7[4 + sl], lc)
        if t == 0:
            TE.wait(sS, sgn_done)
        m7 = 0
        for eh in range(2):
            bnk = (4 + sl) if eh == 0 else (6 + sl)
            TE.wait(sV, f7_banks[bnk])
            for ec in range(8):
                m7 = TE.op(sT, lambda e, o=pb[bnk][:, 0:512],
                           l=ap3(xnl7[:], sl * 1024 + ec * 128, [[2 * 1024, 128], [1, 128]]),
                           r=ap3(sgnb[:], ec * DM + eh * 512, [[8 * DM, 128], [1, 512]]),
                           st=(ec == 0), sp=(ec == 7): e.matmul(o, l, r, start=st, stop=sp))
        VE.wait(sL7[sl], la)
        VE.wait(sL7[2 + sl], lb)
        VE.wait(sO7[sl], o7_marks[sl])
        VE.wait(sT, m7)
        VE.opw(sV, lambda e, o=x27[:, bb:bb + DM], a=x17[:, bb:bb + DM], b_=art7[:, bb:bb + DM]:
               e.tensor_tensor(o, a, b_, ALU.add))
        VE.opw(sV, lambda e, o=x27[:, bb:bb + DM]: e.tensor_tensor(o, o, bob[:], ALU.add))
        for eh in range(2):
            bnk = (4 + sl) if eh == 0 else (6 + sl)
            vm7 = VE.opw(sV, lambda e, o=x27[:, bb + eh * 512:bb + (eh + 1) * 512],
                         s=pb[bnk][:, 0:512]: e.tensor_tensor(o, o, s, ALU.add))
            f7_banks[bnk] = vm7
        l7_con[sl] = vm7
        SC.wait(sV, vm7)
        SC.wait(sO7[sl], o7_marks[sl])
        om = SC.dma(sO7[sl], out_my[t * 128:(t + 1) * 128, :], x27[:, bb:bb + DM])
        o7_marks[sl] = om
    out_tot = (sO7[0].n, sO7[1].n)


    end_sems = [sConst, sW0, sPC, sXL[0], sXL[1], sXS[0], sXS[1], sTap, sTapB]
    for s_ in end_sems:
        SY.wait(s_, s_.n)

    for ph in reversed(list(ph_stacks)):
        close_phase(ph)

    with nc.Block() as block:
        block.sync(lambda e: [f(e) for f in SY.ops])
        block.scalar(lambda e: [f(e) for f in SC.ops])
        block.vector(lambda e: [f(e) for f in VE.ops])
        block.tensor(lambda e: [f(e) for f in TE.ops])
        block.gpsimd(lambda e: [f(e) for f in GP.ops])

    stack.close()
    return nc, tap_out


def prepare_inputs(inputs):
    x = np.ascontiguousarray(inputs["L0_attn_x"])
    f = np.float32
    w_inT_ = np.ascontiguousarray(inputs["L0_attn_W_in"].T)
    kcat_ = np.ascontiguousarray(
        np.concatenate([inputs["L0_attn_Kx"][:, 0, :], inputs["L0_attn_Kz"][:, 0, :]], 0))
    w_xprojT_ = np.ascontiguousarray(inputs["L0_attn_W_xproj"].T)
    w_dtT = np.ascontiguousarray(inputs["L0_attn_W_dt"].T)
    woutT = np.ascontiguousarray(inputs["L0_attn_W_out"].T)
    ident = np.eye(128, dtype=f)
    import ml_dtypes
    identb_ = ident.astype(ml_dtypes.bfloat16)

    maps = []
    for k in range(8):
        b, pp = k // 2, k % 2
        x_ext_ = np.zeros((EXT, DM), f)
        x_ext_[1:TOK + 1] = x[b, pp * TOK:(pp + 1) * TOK]
        if pp == 1:
            x_ext_[0] = x[b, TOK - 1]
        if pp == 0:
            x_ext_[TOK + 1] = x[b, TOK]
        c0 = pp * 256
        A_sl = np.asarray(inputs["L0_attn_A_log"][c0:c0 + 256], f)
        A_arr_ = np.zeros((128, 16), f)
        D_arr_ = np.zeros((128, 2), f)
        for ch in range(2):
            A_arr_[:, ch * 8:(ch + 1) * 8] = A_sl[ch * 128:(ch + 1) * 128]
            D_arr_[:, ch] = np.asarray(inputs["L0_attn_D"][c0 + ch * 128:c0 + (ch + 1) * 128], f)
        hs = pp * 512
        m = {
            "x_ext": x_ext_,
            "w_inT": w_inT_,
            "kcat": kcat_,
            "w_xprojT": w_xprojT_,
            "w_dtT_my": np.ascontiguousarray(w_dtT[:, c0:c0 + 256]),
            "b_dt_my": np.ascontiguousarray(np.asarray(inputs["L0_attn_b_dt"], f)[c0:c0 + 256]),
            "b_dt_arr": np.ascontiguousarray(
                np.asarray(inputs["L0_attn_b_dt"], f)[c0:c0 + 256].reshape(2, 128).T),
            "A_arr": A_arr_,
            "D_arr": D_arr_,
            "woutT_y": np.ascontiguousarray(woutT[c0:c0 + 256]),
            "woutT_z": np.ascontiguousarray(woutT[512:1024]),
            "wg": np.asarray(inputs["L0_ffn_Wg"], f),
            "ws1": np.asarray(inputs["L0_ffn_Ws1"], f),
            "ws3": np.asarray(inputs["L0_ffn_Ws3"], f),
            "ws2": np.asarray(inputs["L0_ffn_Ws2"], f),
            "we1": np.asarray(inputs["L0_ffn_We1"], f),
            "we3": np.asarray(inputs["L0_ffn_We3"], f),
            "we2": np.asarray(inputs["L0_ffn_We2"], f),
            "wq_f": np.asarray(inputs["L1_attn_Wq"], f),
            "wk_f": np.asarray(inputs["L1_attn_Wk"], f),
            "RT": np.ascontiguousarray(np.asarray(inputs["L1_attn_R"], f).T),
            "bqhT": np.ascontiguousarray(np.asarray(inputs["L1_attn_bq"], f).reshape(16, 64).T),
            "bkhT": np.ascontiguousarray(np.asarray(inputs["L1_attn_bk"], f).reshape(16, 64).T),
            "wqT_f": np.ascontiguousarray(np.asarray(inputs["L1_attn_Wq"], f).T),
            "wkT_f": np.ascontiguousarray(np.asarray(inputs["L1_attn_Wk"], f).T),
            "bqh_my": np.ascontiguousarray(
                np.asarray(inputs["L1_attn_bq"], f)[hs:hs + 512].reshape(8, 64).T),
            "bkh_my": np.ascontiguousarray(
                np.asarray(inputs["L1_attn_bk"], f)[hs:hs + 512].reshape(8, 64).T),
            "wq_my": np.ascontiguousarray(np.asarray(inputs["L1_attn_Wq"], f)[:, hs:hs + 512]),
            "wk_my": np.ascontiguousarray(np.asarray(inputs["L1_attn_Wk"], f)[:, hs:hs + 512]),
            "wv_my": np.ascontiguousarray(np.asarray(inputs["L1_attn_Wv"], f)[:, hs:hs + 512]),
            "bq_my": np.ascontiguousarray(np.asarray(inputs["L1_attn_bq"], f)[hs:hs + 512]),
            "bk_my": np.ascontiguousarray(np.asarray(inputs["L1_attn_bk"], f)[hs:hs + 512]),
            "bv_my": np.ascontiguousarray(np.asarray(inputs["L1_attn_bv"], f)[hs:hs + 512]),
            "wo_my": np.ascontiguousarray(np.asarray(inputs["L1_attn_Wo"], f)[hs:hs + 512]),
            "bo": np.asarray(inputs["L1_attn_bo"], f),
            "w_ffnT": np.ascontiguousarray(np.asarray(inputs["L1_ffn_weight"], f).T),
            "flagv": np.ascontiguousarray(
                np.stack([np.full(128, 1.0 - pp, f), np.full(128, float(pp), f)], 1)),
            "ident_f": ident,
            "ident_b": identb_,
        }
        maps.append(m)
    return maps


def kernel(**inputs):
    maps = prepare_inputs(inputs)
    nc, tap_out = build(TAPS)
    res = run_bass_kernel_spmd(nc, maps, core_ids=list(range(8)))
    kernel.last = res
    out = np.empty((4, SEQ, DM), np.float32)
    for k in range(8):
        b, pp = k // 2, k % 2
        out[b, pp * TOK:(pp + 1) * TOK] = res.results[k]["out_my"]
    return out


# version 07
# speedup vs baseline: 1.4998x; 1.4998x over previous
import contextlib
import numpy as np

import concourse.bass as bass
import concourse.mybir as mybir
from concourse.bass_utils import run_bass_kernel_spmd

F32 = mybir.dt.float32
BF16 = mybir.dt.bfloat16
AF = mybir.ActivationFunctionType
ALU = mybir.AluOpType
AX = mybir.AxisListType.X

TOK = 4096
EXT = 4224
NT128 = 33
SEQ = 8192
DM = 1024
FF = 2048
NE = 8
PAIRS = [[0, 1], [2, 3], [4, 5], [6, 7]]

TAPS = ()
SKIP_PRECAST = False
DELAY_GP = False


class Sem:
    def __init__(self, stack, nc, name):
        self.h = stack.enter_context(nc.semaphore(name))
        self.n = 0


class Q:

    def __init__(self, name):
        self.name = name
        self.ops = []

    def add(self, f):
        self.ops.append(f)

    def wait(self, sem, val):
        if val > 0:
            self.ops.append(lambda e: e.wait_ge(sem.h, val))

    def dma(self, sem, out, in_):
        sem.n += 16
        self.ops.append(lambda e: e.dma_start(out=out, in_=in_).then_inc(sem.h, 16))
        return sem.n

    def op(self, sem, f):
        sem.n += 1
        self.ops.append(lambda e: f(e).then_inc(sem.h, 1))
        return sem.n

    def opw(self, sem, f):
        m = self.op(sem, f)
        self.wait(sem, m)
        return m


def ap3(base_ap, offset_elems, dims):
    return bass.AP(base_ap.tensor, base_ap.offset + offset_elems, dims)


def build(taps=()):
    nc = bass.Bass("TRN2", target_bir_lowering=False, debug=False)
    stack = contextlib.ExitStack()

    def sbuf(name, shape, dtype=F32):
        return stack.enter_context(nc.sbuf_tensor(name, shape, dtype))

    ph_stacks = {}

    def psbuf(ph, name, shape, dtype=F32):
        st = ph_stacks.setdefault(ph, contextlib.ExitStack())
        return st.enter_context(nc.sbuf_tensor(name, shape, dtype))

    def close_phase(ph):
        if ph in ph_stacks:
            ph_stacks.pop(ph).close()

    pb = [stack.enter_context(nc.psum_tensor("pb%d" % i, [128, 512], F32)) for i in range(4)]
    psA = stack.enter_context(nc.psum_tensor("psA", [128, 1024], F32))
    psB = stack.enter_context(nc.psum_tensor("psB", [128, 1024], F32))
    pb = pb + [psA[:, 0:512], psA[:, 512:1024], psB[:, 0:512], psB[:, 512:1024]]

    def din(name, shape, dtype=F32):
        return nc.dram_tensor(name, shape, dtype, kind="ExternalInput").ap()

    def dint(name, shape, dtype=F32):
        return nc.dram_tensor(name, shape, dtype).ap()

    x_ext = din("x_ext", [EXT, DM])
    w_inT = din("w_inT", [DM, DM])
    kcat = din("kcat", [DM, 3])
    w_xprojT = din("w_xprojT", [512, 80])
    w_dtT_my = din("w_dtT_my", [64, 256])
    b_dt_my = din("b_dt_my", [256])
    b_dt_arr = din("b_dt_arr", [128, 2])
    A_arr = din("A_arr", [128, 16])
    D_arr = din("D_arr", [128, 2])
    woutT_y = din("woutT_y", [256, DM])
    woutT_z = din("woutT_z", [512, DM])
    wg = din("wg", [DM, 8])
    ws1 = din("ws1", [DM, FF])
    ws3 = din("ws3", [DM, FF])
    ws2 = din("ws2", [FF, DM])
    we1 = din("we1", [NE, DM, FF])
    we3 = din("we3", [NE, DM, FF])
    we2 = din("we2", [NE, FF, DM])
    wq_f = din("wq_f", [DM, DM])
    wk_f = din("wk_f", [DM, DM])
    RT = din("RT", [64, 8])
    bqhT = din("bqhT", [64, 16])
    bkhT = din("bkhT", [64, 16])
    wqT_f = din("wqT_f", [DM, DM])
    wkT_f = din("wkT_f", [DM, DM])
    bqh_my = din("bqh_my", [64, 8])
    bkh_my = din("bkh_my", [64, 8])
    wq_my = din("wq_my", [DM, 512])
    wk_my = din("wk_my", [DM, 512])
    wv_my = din("wv_my", [DM, 512])
    bq_my = din("bq_my", [512])
    bk_my = din("bk_my", [512])
    bv_my = din("bv_my", [512])
    wo_my = din("wo_my", [512, DM])
    bo = din("bo", [DM])
    w_ffnT = din("w_ffnT", [DM, DM])
    flagv = din("flagv", [128, 2])
    ident_f = din("ident_f", [128, 128])
    ident_b = din("ident_b", [128, 128], BF16)

    out_my = nc.dram_tensor("out_my", [TOK, DM], F32, kind="ExternalOutput").ap()

    XN0 = dint("XN0", [DM, EXT])
    ZB = dint("ZB", [512, TOK], BF16)
    a2a_in = dint("a2a_in", [1024, TOK], BF16)
    agxb_out = dint("agxb_out", [2048, TOK], BF16)
    agp_in = dint("agp_in", [TOK, 80])
    agp_out = dint("agp_out", [SEQ, 80])
    BCD = dint("BCD", [16, SEQ])
    prt_in = dint("prt_in", [2, TOK, DM])
    prt_out = dint("prt_out", [TOK, DM])
    X1 = dint("X1", [TOK, DM])
    agx_in = dint("agx_in", [1280, TOK], BF16)
    agx_out = dint("agx_out", [2560, TOK], BF16)
    art_in = dint("art_in", [2, TOK, DM])
    art_out = dint("art_out", [TOK, DM])

    tap_shapes = {
        "xn0": ([DM, EXT], F32), "zb": ([512, TOK], BF16),
        "a2a_in": ([1024, TOK], BF16), "proj": ([TOK, 80], F32),
        "a2a_out": ([2048, TOK], BF16), "agp_out": ([SEQ, 80], F32),
        "yT": ([2 * 128, SEQ], BF16), "delta": ([2 * 128, SEQ], F32),
        "bc": ([16, SEQ], F32),
        "prt_out": ([TOK, DM], F32), "x1": ([TOK, DM], F32),
        "agx_out": ([2560, TOK], BF16), "art_out": ([TOK, DM], F32),
        "wp": ([8, TOK], F32), "glog": ([128, 32 * 8], F32),
        "xntm": ([NT128, 128, DM], F32),
        "xraw": ([NT128, 128, DM], F32),
        "lnstats": ([NT128, 128, 16], F32),
    }
    tap_out = {}
    for t in taps:
        sh, dt = tap_shapes[t]
        tap_out[t] = nc.dram_tensor("tap_" + t, sh, dt, kind="ExternalOutput").ap()

    identf = sbuf("identf", [128, 128], F32)
    identb = sbuf("identb", [128, 128], BF16)
    kcol = sbuf("kcol", [128, 8 * 3], F32)
    wpTok = sbuf("wpTok", [128, 32 * 8], F32)
    yTs = psbuf("py", "yTs", [128, 2 * SEQ], BF16)
    dtrawT = psbuf("pdt", "dtrawT", [64, SEQ], BF16)
    w_inTb = psbuf("pw", "w_inTb", [128, 8 * DM], BF16)
    wxpTb = psbuf("pw", "wxpTb", [128, 4 * 80], BF16)

    SY = Q("sync")
    SC = Q("scalar")
    VE = Q("vector")
    TE = Q("tensor")
    GP = Q("gpsimd")

    def mksem(name):
        return Sem(stack, nc, name)

    def mksems(name, k):
        return [Sem(stack, nc, "%s%d" % (name, j)) for j in range(k)]

    sT = mksem("sT")
    sV = mksem("sV")
    sS = mksem("sS")
    sG = mksem("sG")

    ALLQ = []

    def fence_all(marks):
        for q_ in ALLQ:
            for se, v in marks:
                q_.wait(se, v)

    ALLQ.extend([SY, SC, VE, TE, GP])
    _delay_sem = [None, 0]
    sConst = mksem("sConst")
    SY.dma(sConst, identf[:], ident_f[:])
    SY.dma(sConst, identb[:], ident_b[:])
    SY.dma(sConst, ap3(kcol[:], 0, [[24, 128], [3, 8], [1, 3]]),
           ap3(kcat, 0, [[3, 128], [384, 8], [1, 3]]))
    p0_const = sConst.n

    sW0 = mksem("sW0")
    if DELAY_GP:
        GP.add(lambda e: e.wait_ge(_delay_sem[0].h, _delay_sem[1]))
    for ec in range(8):
        GP.dma(sW0, w_inTb[:, ec * DM:(ec + 1) * DM], w_inT[ec * 128:(ec + 1) * 128, :])
    for c in range(4):
        GP.dma(sW0, wxpTb[:, c * 80:(c + 1) * 80], w_xprojT[c * 128:(c + 1) * 128, :])
    p0_small = sW0.n
    sPC = mksem("sPC")
    moe_cast_done = sPC.n

    xt = psbuf("p1a", "xt", [128, 2 * DM], F32)
    stg = psbuf("p1a", "stg", [128, 2 * DM], F32)
    xnt = psbuf("p1a", "xnt", [128, 2 * DM], F32)
    st_ = psbuf("p1a", "st_", [128, 16], F32)

    sTapB = mksem("sTapB")
    sXL = mksems("sXL", 2)
    sXS = mksems("sXS", 2)
    ln_store = []
    nrm_mark = []
    trl_mark = []
    ev_hist = []
    for i in range(NT128):
        bb = (i % 2) * DM
        sl = i % 2
        xti = xt[:, bb:bb + DM]
        if i >= 2:
            SY.wait(sV, nrm_mark[i - 2])
        ld = SY.dma(sXL[sl], xti, x_ext[i * 128:(i + 1) * 128, :])

        VE.wait(sXL[sl], ld)
        if "xraw" in tap_out:
            GP.wait(sXL[sl], ld)
            xrd = GP.dma(sTapB, tap_out["xraw"][i], xti)
            SY.wait(sTapB, xrd)
        if i >= 2:
            VE.wait(sT, trl_mark[i - 2])
        VE.add(lambda e, a=st_[:, 0:1], b_=xti: e.reduce_sum(a, b_, axis=AX))
        VE.opw(sV, lambda e, a=stg[:, bb:bb + DM], b_=xti: e.tensor_tensor(a, b_, b_, ALU.mult))
        VE.opw(sV, lambda e, a=st_[:, 1:2], b_=stg[:, bb:bb + DM]: e.reduce_sum(a, b_, axis=AX))
        VE.add(lambda e: e.tensor_scalar(st_[:, 2:3], st_[:, 0:1], 1.0 / DM, None, ALU.mult))
        VE.opw(sV, lambda e: e.tensor_scalar(st_[:, 3:4], st_[:, 1:2], 1.0 / DM, None, ALU.mult))
        VE.opw(sV, lambda e: e.tensor_tensor(st_[:, 4:5], st_[:, 2:3], st_[:, 2:3], ALU.mult))
        VE.opw(sV, lambda e: e.tensor_tensor(st_[:, 5:6], st_[:, 3:4], st_[:, 4:5], ALU.subtract))
        var_m = VE.op(sV, lambda e: e.tensor_scalar(st_[:, 5:6], st_[:, 5:6], 1e-5, None, ALU.add))
        SC.wait(sV, var_m)
        sq_m = SC.op(sS, lambda e: e.sqrt(st_[:, 6:7], st_[:, 5:6]))
        VE.wait(sS, sq_m)
        rc_m = VE.op(sV, lambda e: e.reciprocal(st_[:, 7:8], st_[:, 6:7]))
        VE.wait(sV, rc_m)
        nm = VE.op(sV, lambda e, o=stg[:, bb:bb + DM], x_=xti: e.tensor_scalar(
            o, x_, st_[:, 2:3], st_[:, 7:8], ALU.subtract, ALU.mult))
        nrm_mark.append(nm)
        if "lnstats" in tap_out:
            GP.wait(sV, nm)
            GP.dma(sTapB, tap_out["lnstats"][i], st_[:])
        if "xntm" in tap_out:
            GP.wait(sV, nm)
            GP.dma(sTapB, tap_out["xntm"][i], stg[:, bb:bb + DM])

        TE.wait(sV, nm)
        if i == 0:
            TE.wait(sConst, p0_const)
        if i >= 2:
            SC.wait(sXS[sl], ln_store[i - 2])
        tm = 0
        for c in range(8):
            bank = pb[c % 2]
            if len(ev_hist) >= 2:
                TE.wait(sS, ev_hist[-2])
            tm = TE.op(sT, lambda e, o=bank[:, 0:128], s=stg[:, bb + c * 128:bb + (c + 1) * 128]:
                       e.transpose(o, s, identf[:]))
            SC.wait(sT, tm)
            em = SC.op(sS, lambda e, o=xnt[:, bb + c * 128:bb + (c + 1) * 128], s=bank[:, 0:128]:
                       e.copy(o, s))
            ev_hist.append(em)
        trl_mark.append(tm)
        SC.wait(sS, ev_hist[-1])
        stm = SC.dma(sXS[sl],
                     ap3(XN0, i * 128, [[EXT, 128], [128 * EXT, 8], [1, 128]]),
                     ap3(xnt[:], bb, [[2 * DM, 128], [128, 8], [1, 128]]))
        ln_store.append(stm)
    _p1a_tot = (sXS[0].n, sXS[1].n)
    _delay_sem[0], _delay_sem[1] = sXS[0], _p1a_tot[0]

    def wait_p1a(q):
        q.wait(sXS[0], _p1a_tot[0])
        q.wait(sXS[1], _p1a_tot[1])

    sTap = mksem("sTap")
    if "xn0" in tap_out:
        wait_p1a(SY)
        SY.dma(sTap, tap_out["xn0"][:], XN0[:])

    fence_all([(sXS[0], _p1a_tot[0]), (sXS[1], _p1a_tot[1]),
               (sV, sV.n), (sS, sS.n), (sT, sT.n),
               (sXL[0], sXL[0].n), (sXL[1], sXL[1].n)])
    close_phase("p1a")

    xnb = psbuf("p1b", "xnb", [128, 2 * 8 * 512], BF16)
    hbuf = psbuf("p1b", "hbuf", [128, 8 * 516], F32)
    cvb = psbuf("p1b", "cvb", [128, 512], F32)
    sgb = psbuf("p1b", "sgb", [128, 512], F32)
    xbt2 = psbuf("p1b", "xbt2", [128, 2 * 512], BF16)

    sXNB = mksems("sXNB", 2)
    sBS = mksems("sBS", 2)
    p1b_tiles = [(i * 512, 512) for i in range(8)] + [(4096, 128)]

    TE.wait(sW0, p0_small)
    VE.wait(sConst, p0_const)
    ev_all = ev_hist[-1]
    TE.wait(sS, ev_all)
    bank_free = [ev_all] * 8
    conv_done = 0
    tail_mark = 0
    sb_store = [0, 0]
    ht_marks = {}
    p1b_store_marks = []
    for ti, (t0, W) in enumerate(p1b_tiles):
        sl = ti % 2
        xnbs = xnb[:, sl * 4096:(sl + 1) * 4096]
        if ti >= 2:
            GP.wait(sT, ht_marks[(ti - 2, 7)])
        need_tile = (t0 + W + 127) // 128 - 1
        GP.wait(sXS[need_tile % 2], ln_store[need_tile])
        if need_tile >= 1:
            GP.wait(sXS[(need_tile - 1) % 2], ln_store[need_tile - 1])
        ldx = GP.dma(sXNB[sl], ap3(xnb[:], sl * 4096, [[2 * 8 * 512, 128], [512, 8], [1, W]]),
                     ap3(XN0, t0, [[EXT, 128], [128 * EXT, 8], [1, W]]))
        TE.wait(sXNB[sl], ldx)
        for cs in range(8):
            TE.wait(sS, bank_free[cs])
            for ec in range(8):
                mm = TE.op(sT, lambda e, o=pb[cs][:, 0:W],
                           l=w_inTb[:, ec * DM + cs * 128:ec * DM + (cs + 1) * 128],
                           r=ap3(xnb[:], sl * 4096 + ec * 512, [[2 * 8 * 512, 128], [1, W]]),
                           st=(ec == 0), sp=(ec == 7):
                           e.matmul(o, l, r, start=st, stop=sp))
            ht_marks[(ti, cs)] = mm
        if ti >= 1:
            SC.wait(sV, conv_done)
        ev_cs = []
        for cs in range(8):
            SC.wait(sT, ht_marks[(ti, cs)])
            em = SC.op(sS, lambda e, o=hbuf[:, cs * 516 + 2:cs * 516 + 2 + W], s=pb[cs][:, 0:W]:
                       e.copy(o, s))
            ev_cs.append(em)
            bank_free[cs] = em
        if ti == 0:
            jlo, jhi = 1, 511
        elif W == 512:
            jlo, jhi = t0 - 1, t0 + 511
        else:
            jlo, jhi = 4095, 4097
        Wp = jhi - jlo
        st_marks_tile = []
        for cs in range(8):
            VE.wait(sS, ev_cs[cs])
            if ti > 0:
                VE.wait(sV, tail_mark)
            base = jlo - t0 + 2
            h0 = hbuf[:, cs * 516 + base - 1:cs * 516 + base - 1 + Wp]
            h1 = hbuf[:, cs * 516 + base:cs * 516 + base + Wp]
            h2 = hbuf[:, cs * 516 + base + 1:cs * 516 + base + 1 + Wp]
            VE.opw(sV, lambda e, o=cvb[:, 0:Wp], a=h0, k=kcol[:, cs * 3:cs * 3 + 1]:
                   e.tensor_scalar(o, a, k, None, ALU.mult))
            VE.opw(sV, lambda e, o=cvb[:, 0:Wp], a=h1, k=kcol[:, cs * 3 + 1:cs * 3 + 2]:
                   e.scalar_tensor_tensor(o, a, k, o, ALU.mult, ALU.add))
            cm = VE.opw(sV, lambda e, o=cvb[:, 0:Wp], a=h2, k=kcol[:, cs * 3 + 2:cs * 3 + 3]:
                        e.scalar_tensor_tensor(o, a, k, o, ALU.mult, ALU.add))
            SC.wait(sV, cm)
            sgm = SC.op(sS, lambda e, o=sgb[:, 0:Wp], s=cvb[:, 0:Wp]:
                        e.activation(o, s, AF.Sigmoid))
            VE.wait(sS, sgm)
            ssl = cs % 2
            VE.wait(sBS[ssl], sb_store[ssl])
            xm = VE.opw(sV, lambda e, o=xbt2[:, ssl * 512:ssl * 512 + Wp],
                        a=cvb[:, 0:Wp], b_=sgb[:, 0:Wp]: e.tensor_tensor(o, a, b_, ALU.mult))
            SC.wait(sV, xm)
            dst = a2a_in[cs * 128:(cs + 1) * 128, jlo - 1:jhi - 1]
            sm = SC.dma(sBS[ssl], dst, xbt2[:, ssl * 512:ssl * 512 + Wp])
            sb_store[ssl] = sm
            st_marks_tile.append(sm)
        if ti < len(p1b_tiles) - 1:
            for cs in range(7):
                VE.add(lambda e, o=hbuf[:, cs * 516:cs * 516 + 2],
                       s=hbuf[:, cs * 516 + W:cs * 516 + W + 2]: e.tensor_copy(o, s))
            tail_mark = VE.op(sV, lambda e, o=hbuf[:, 7 * 516:7 * 516 + 2],
                              s=hbuf[:, 7 * 516 + W:7 * 516 + W + 2]: e.tensor_copy(o, s))
        conv_done = sV.n
        p1b_store_marks.append(list(st_marks_tile))
    p1b_bs_tot = (sBS[0].n, sBS[1].n)

    if "a2a_in" in tap_out:
        SY.wait(sBS[0], p1b_bs_tot[0])
        SY.wait(sBS[1], p1b_bs_tot[1])
        SY.dma(sTap, tap_out["a2a_in"][:], a2a_in[:])
    if "zb" in tap_out:
        SY.wait(sBS[0], p1b_bs_tot[0])
        SY.wait(sBS[1], p1b_bs_tot[1])
        SY.dma(sTap, tap_out["zb"][:], a2a_in[512:1024, :])

    xbl = psbuf("p1c", "xbl", [128, 2 * 4 * 128], BF16)
    pjs = psbuf("p1c", "pjs", [128, 2 * 80], F32)
    sPJL = mksems("sPJL", 2)
    sPJS = mksems("sPJS", 2)
    SY.wait(sBS[0], p1b_bs_tot[0])
    SY.wait(sBS[1], p1b_bs_tot[1])
    TE.wait(sW0, p0_small)
    pj_ev = {2: bank_free[2], 3: bank_free[3]}
    pj_mm = [0, 0]
    pjs_marks = [0, 0]
    for t in range(32):
        sl = t % 2
        if t >= 2:
            SY.wait(sT, pj_mm[sl])
        ld = SY.dma(sPJL[sl],
                    ap3(xbl[:], sl * 512, [[2 * 512, 128], [128, 4], [1, 128]]),
                    ap3(a2a_in, t * 128, [[TOK, 128], [128 * TOK, 4], [1, 128]]))
        bank = 2 + (t % 2)
        TE.wait(sPJL[sl], ld)
        TE.wait(sS, pj_ev[bank])
        for cc in range(4):
            mm = TE.op(sT, lambda e, o=pb[bank][:, 0:80],
                       l=ap3(xbl[:], sl * 512 + cc * 128, [[2 * 512, 128], [1, 128]]),
                       r=wxpTb[:, cc * 80:(cc + 1) * 80],
                       st=(cc == 0), sp=(cc == 3): e.matmul(o, l, r, start=st, stop=sp))
        pj_mm[sl] = mm
        SC.wait(sT, mm)
        SC.wait(sPJS[sl], pjs_marks[sl])
        em = SC.op(sS, lambda e, o=pjs[:, sl * 80:(sl + 1) * 80], s=pb[bank][:, 0:80]:
                   e.copy(o, s))
        pj_ev[bank] = em
        SC.wait(sS, em)
        sm = SC.dma(sPJS[sl], agp_in[t * 128:(t + 1) * 128, :], pjs[:, sl * 80:(sl + 1) * 80])
        pjs_marks[sl] = sm
    p1c_tot = (sPJS[0].n, sPJS[1].n)
    if "proj" in tap_out:
        SY.wait(sPJS[0], p1c_tot[0])
        SY.wait(sPJS[1], p1c_tot[1])
        SY.dma(sTap, tap_out["proj"][:], agp_in[:])

    sCCp = mksem("sCCp")
    sCCx = mksem("sCCx")
    GP.wait(sPJS[0], p1c_tot[0])
    GP.wait(sPJS[1], p1c_tot[1])
    GP.op(sCCp, lambda e: e.collective_compute(
        "AllGather", mybir.AluOpType.bypass, replica_groups=PAIRS,
        ins=[agp_in[:]], outs=[agp_out[:]]))
    GP.wait(sBS[0], p1b_bs_tot[0])
    GP.wait(sBS[1], p1b_bs_tot[1])
    GP.op(sCCx, lambda e: e.collective_compute(
        "AllGather", mybir.AluOpType.bypass, replica_groups=PAIRS,
        ins=[a2a_in[:]], outs=[agxb_out[:]]))
    cc_proj, cc_xb = sCCp.n, sCCx.n

    if "agp_out" in tap_out:
        SY.wait(sCCp, cc_proj)
        SY.dma(sTap, tap_out["agp_out"][:], agp_out[:])
    if "a2a_out" in tap_out:
        SY.wait(sCCx, cc_xb)
        SY.dma(sTap, tap_out["a2a_out"][:], agxb_out[:])

    sAGL = mksems("sAGL", 2)
    sBCS = mksems("sBCS", 2)
    agl = psbuf("p2", "agl", [128, 2 * 80], F32)
    bcStage = psbuf("p2", "bcStage", [16, 2 * 128], F32)
    SY.wait(sCCp, cc_proj)
    ag_banks = {4: bank_free[4], 5: bank_free[5]}
    agl_marks = [0, 0]
    bcs_marks = [0, 0]
    bc_last = 0
    for t in range(64):
        sl = t % 2
        if t >= 2:
            SY.wait(sT, agl_marks[sl])
        ld = SY.dma(sAGL[sl], agl[:, sl * 80:(sl + 1) * 80], agp_out[t * 128:(t + 1) * 128, :])
        bank = 4 + (t % 2)
        TE.wait(sAGL[sl], ld)
        TE.wait(sS, ag_banks[bank])
        tm = TE.op(sT, lambda e, o=pb[bank][0:80, 0:128], s=agl[:, sl * 80:(sl + 1) * 80]:
                   e.transpose(o, s, identf[:]))
        agl_marks[sl] = tm
        SC.wait(sT, tm)
        SC.add(lambda e, o=ap3(dtrawT[:], t * 128, [[SEQ, 64], [1, 128]]),
               s=pb[bank][0:64, 0:128]: e.copy(o, s))
        SC.wait(sBCS[sl], bcs_marks[sl])
        em = SC.op(sS, lambda e, o=bcStage[:, sl * 128:(sl + 1) * 128],
                   s=pb[bank][64:80, 0:128]: e.activation(o, s, AF.Tanh))
        ag_banks[bank] = em
        SC.wait(sS, em)
        bm = SC.dma(sBCS[sl], ap3(BCD, t * 128, [[SEQ, 16], [1, 128]]),
                    ap3(bcStage[:], sl * 128, [[2 * 128, 16], [1, 128]]))
        bcs_marks[sl] = bm
        bc_last = em
    p2_done = bc_last
    p2_bcs_tot = (sBCS[0].n, sBCS[1].n)
    if "bc" in tap_out:
        SY.wait(sBCS[0], p2_bcs_tot[0])
        SY.wait(sBCS[1], p2_bcs_tot[1])
        SY.dma(sTap, tap_out["bc"][:], BCD[:])

    fence_all([(sV, sV.n), (sS, sS.n), (sT, sT.n),
               (sBS[0], sBS[0].n), (sBS[1], sBS[1].n),
               (sPJL[0], sPJL[0].n), (sPJL[1], sPJL[1].n),
               (sPJS[0], sPJS[0].n), (sPJS[1], sPJS[1].n),
               (sAGL[0], sAGL[0].n), (sAGL[1], sAGL[1].n),
               (sXNB[0], sXNB[0].n), (sXNB[1], sXNB[1].n),
               (sBCS[0], sBCS[0].n), (sBCS[1], sBCS[1].n),
               (sCCp, sCCp.n), (sCCx, sCCx.n)])
    close_phase("p2")
    close_phase("p1c")
    close_phase("p1b")
    close_phase("pw")

    sW3 = mksem("sW3")
    carry = psbuf("pdt", "carry", [128, 16], F32)
    flg = psbuf("pdt", "flg", [128, 2], F32)
    w_dtTb = psbuf("pdt", "w_dtTb", [64, 256], BF16)
    abuf = psbuf("pdt", "abuf", [128, 16 * 4], F32)
    dcol = psbuf("pdt", "dcol", [128, 2], F32)
    bdt2 = psbuf("pdt", "bdt2", [128, 2], F32)
    GP.dma(sW3, w_dtTb[:], w_dtT_my[:])
    GP.dma(sW3, abuf[:, 0:16], A_arr[:])
    GP.dma(sW3, dcol[:], D_arr[:])
    GP.dma(sW3, bdt2[:], b_dt_arr[:])
    GP.dma(sW3, flg[:], flagv[:])
    w3_tot = sW3.n
    SC.wait(sW3, w3_tot)
    e1 = SC.op(sS, lambda e: e.activation(abuf[:, 16:32], abuf[:, 0:16], AF.Exp))
    VE.wait(sS, e1)
    v1 = VE.opw(sV, lambda e: e.tensor_scalar(abuf[:, 16:32], abuf[:, 16:32], 1.0, None, ALU.add))
    SC.wait(sV, v1)
    e2 = SC.op(sS, lambda e: e.activation(abuf[:, 16:32], abuf[:, 16:32], AF.Ln))
    VE.wait(sS, e2)
    VE.opw(sV, lambda e: e.tensor_scalar(abuf[:, 16:32], abuf[:, 16:32], 1e-4, None, ALU.add))
    aprep = VE.opw(sV, lambda e: e.tensor_scalar(abuf[:, 32:48], abuf[:, 16:32], -1.0, None, ALU.mult))

    TC = 2048
    delta_sb = psbuf("p3", "delta_sb", [128, TC], F32)
    xbsel = psbuf("p3", "xbsel", [128, TC], F32)
    decb = psbuf("p3", "decb", [128, TC], F32)
    ub = psbuf("p3", "ub", [128, TC], F32)
    drv = psbuf("p3", "drv", [128, TC], F32)
    sout = psbuf("p3", "sout", [128, TC], F32)
    yacc = psbuf("p3", "yacc", [128, TC], F32)
    tmp3 = psbuf("p3", "tmp3", [128, TC], F32)
    brep = psbuf("p3", "brep", [128, 2 * TC], F32)
    crep = psbuf("p3", "crep", [128, 2 * TC], F32)
    xbc = psbuf("p3", "xbc", [128, 2 * TC], BF16)
    sBR = mksems("sBR", 2)
    sXBL = mksems("sXBL", 2)
    TE.wait(sS, p2_done)
    TE.wait(sW3, sW3.n)
    VE.wait(sW3, sW3.n)
    dl_banks = {6: bank_free[6], 7: bank_free[7]}
    brc_mark = [0, 0]
    sel_prev = [0]
    ych_marks = []
    for tc in range(4):
        th, lc = tc // 2, (tc % 2) * TC
        for ch in range(2):
            SY.wait(sV, sel_prev[0])
            xl = SY.dma(sXBL[ch], ap3(xbc[:], 0, [[2 * TC, 128], [TC, 2], [1, TC]]),
                        ap3(agxb_out, th * 1024 * TOK + ch * 128 * TOK + lc,
                            [[TOK, 128], [256 * TOK, 2], [1, TC]]))
            VE.wait(sXBL[ch], xl)
            VE.opw(sV, lambda e: e.tensor_scalar(xbsel[:], xbc[:, 0:TC], flg[:, 0:1], None, ALU.mult))
            selm = VE.opw(sV, lambda e: e.scalar_tensor_tensor(
                xbsel[:], xbc[:, TC:2 * TC], flg[:, 1:2], xbsel[:], ALU.mult, ALU.add))
            pass
            for sub in range(4):
                bank = 6 + (sub % 2)
                TE.wait(sS, dl_banks[bank])
                dm = TE.op(sT, lambda e, o=pb[bank][:, 0:512],
                           l=w_dtTb[:, ch * 128:(ch + 1) * 128],
                           r=ap3(dtrawT[:], tc * TC + sub * 512, [[SEQ, 64], [1, 512]]):
                           e.matmul(o, l, r, start=True, stop=True))
                SC.wait(sT, dm)
                em = SC.op(sS, lambda e, o=delta_sb[:, sub * 512:(sub + 1) * 512],
                           s=pb[bank][:, 0:512], b_=bdt2[:, ch:ch + 1]:
                           e.activation(o, s, AF.Exp, bias=b_))
                dl_banks[bank] = em
                VE.wait(sS, em)
                a1 = VE.op(sV, lambda e, o=delta_sb[:, sub * 512:(sub + 1) * 512]:
                           e.tensor_scalar(o, o, 1.0, None, ALU.add))
                SC.wait(sV, a1)
                lm = SC.op(sS, lambda e, o=delta_sb[:, sub * 512:(sub + 1) * 512]:
                           e.activation(o, o, AF.Ln))
                VE.wait(sS, lm)
                VE.opw(sV, lambda e, o=delta_sb[:, sub * 512:(sub + 1) * 512]:
                       e.tensor_scalar(o, o, 1e-4, 1.0, ALU.max, ALU.min))
            for n in range(8):
                nb = n % 2
                SY.wait(sV, brc_mark[nb])
                br = SY.dma(sBR[nb], ap3(brep[:], nb * TC, [[2 * TC, 128], [0, 1], [1, TC]]),
                            ap3(BCD, n * SEQ + tc * TC, [[0, 128], [0, 1], [1, TC]]))
                cr = SY.dma(sBR[nb], ap3(crep[:], nb * TC, [[2 * TC, 128], [0, 1], [1, TC]]),
                            ap3(BCD, (8 + n) * SEQ + tc * TC, [[0, 128], [0, 1], [1, TC]]))
                VE.wait(sBR[nb], cr)
                acol = abuf[:, 32 + ch * 8 + n:32 + ch * 8 + n + 1]
                dts = VE.opw(sV, lambda e, a=acol: e.tensor_scalar(decb[:], delta_sb[:], a, None, ALU.mult))
                SC.wait(sV, dts)
                ex = SC.op(sS, lambda e: e.activation(decb[:], decb[:], AF.Exp))
                VE.wait(sS, ex)
                VE.opw(sV, lambda e: e.tensor_scalar(decb[:], decb[:], 1e-4, 1.0, ALU.max, ALU.min))
                VE.opw(sV, lambda e, b_=brep[:, nb * TC:(nb + 1) * TC]:
                       e.tensor_tensor(ub[:], xbsel[:], b_, ALU.mult))
                VE.opw(sV, lambda e: e.tensor_tensor(drv[:], decb[:], ub[:], ALU.mult))
                VE.opw(sV, lambda e: e.tensor_tensor(drv[:], ub[:], drv[:], ALU.subtract))
                init = 0.0 if tc == 0 else carry[:, ch * 8 + n:ch * 8 + n + 1]
                VE.opw(sV, lambda e, i_=init: e.tensor_tensor_scan(
                    out=sout[:], data0=decb[:], data1=drv[:], initial=i_,
                    op0=ALU.mult, op1=ALU.add))
                VE.opw(sV, lambda e, o=carry[:, ch * 8 + n:ch * 8 + n + 1]:
                       e.tensor_copy(o, sout[:, TC - 1:TC]))
                cr_ = crep[:, nb * TC:(nb + 1) * TC]
                if n == 0:
                    VE.opw(sV, lambda e, c_=cr_: e.tensor_tensor(yacc[:], sout[:], c_, ALU.mult))
                else:
                    VE.opw(sV, lambda e, c_=cr_: e.tensor_tensor(tmp3[:], sout[:], c_, ALU.mult))
                    VE.opw(sV, lambda e: e.tensor_tensor(yacc[:], yacc[:], tmp3[:], ALU.add))
                brc_mark[nb] = sV.n
            sel_prev[0] = sV.n
            ym = VE.opw(sV, lambda e, d_=dcol[:, ch:ch + 1],
                        o=ap3(yTs[:], ch * SEQ + tc * TC, [[2 * SEQ, 128], [1, TC]]):
                        e.scalar_tensor_tensor(o, xbsel[:], d_, yacc[:], ALU.mult, ALU.add))
            ych_marks.append(ym)
            sel_prev[0] = ym
    p3_done = sV.n
    if "yT" in tap_out:
        VE.wait(sV, p3_done)
        SY.wait(sV, p3_done)
        SY.dma(sTap, ap3(tap_out["yT"], 0, [[SEQ, 128], [1, SEQ]]),
               ap3(yTs[:], 0, [[2 * SEQ, 128], [1, SEQ]]))
        SY.dma(sTap, ap3(tap_out["yT"], 128 * SEQ, [[SEQ, 128], [1, SEQ]]),
               ap3(yTs[:], SEQ, [[2 * SEQ, 128], [1, SEQ]]))

    fence_all([(sV, sV.n), (sS, sS.n), (sT, sT.n),
               (sBR[0], sBR[0].n), (sBR[1], sBR[1].n),
               (sXBL[0], sXBL[0].n), (sXBL[1], sXBL[1].n)])
    close_phase("p3")
    close_phase("pdt")

    sW4 = mksem("sW4")
    wyb = psbuf("p4", "wyb", [128, 2 * DM], BF16)
    wzf = psbuf("p4", "wzf", [128, 4 * DM], F32)
    wzb = psbuf("p4", "wzb", [128, 4 * DM], BF16)
    for c2 in range(2):
        GP.dma(sW4, wyb[:, c2 * DM:(c2 + 1) * DM], woutT_y[c2 * 128:(c2 + 1) * 128, :])
    for c4 in range(4):
        GP.dma(sW4, wzf[:, c4 * DM:(c4 + 1) * DM], woutT_z[c4 * 128:(c4 + 1) * 128, :])
    w4_tot = sW4.n
    VE.wait(sW4, w4_tot)
    wz_m = VE.opw(sV, lambda e: e.tensor_scalar(wzb[:], wzf[:], 0.5, None, ALU.mult))
    zbl = psbuf("p4", "zbl", [128, 2 * 4 * 128], BF16)
    pst = psbuf("p4", "pst", [128, 2 * DM], F32)
    sZL = mksems("sZL", 2)
    sPS = mksems("sPS", 2)
    TE.wait(sV, p3_done)
    TE.wait(sV, wz_m)
    SY.wait(sCCx, cc_xb)
    bank_guard = {0: bank_free[0], 1: bank_free[1], 2: pj_ev[2], 3: pj_ev[3]}
    zl_mark = [0, 0]
    ps_mark = [0, 0]
    prt_marks = []
    for t in range(64):
        th, lt = t // 32, (t % 32) * 128
        sl = t % 2
        if t >= 2:
            SY.wait(sT, zl_mark[sl])
        zl = SY.dma(sZL[sl], ap3(zbl[:], sl * 512, [[1024, 128], [128, 4], [1, 128]]),
                    ap3(agxb_out, th * 1024 * TOK + 512 * TOK + lt,
                        [[TOK, 128], [128 * TOK, 4], [1, 128]]))
        TE.wait(sZL[sl], zl)
        SC.wait(sPS[sl], ps_mark[sl])
        mmz = 0
        for eh in range(2):
            bank = sl * 2 + eh
            TE.wait(sS, bank_guard[bank])
            for c2 in range(2):
                TE.op(sT, lambda e, o=pb[bank][:, 0:512],
                      l=ap3(yTs[:], c2 * SEQ + t * 128, [[2 * SEQ, 128], [1, 128]]),
                      r=wyb[:, c2 * DM + eh * 512:c2 * DM + eh * 512 + 512],
                      st=(c2 == 0), sp=False: e.matmul(o, l, r, start=st, stop=sp))
            for c4 in range(4):
                mmz = TE.op(sT, lambda e, o=pb[bank][:, 0:512],
                            l=ap3(zbl[:], sl * 512 + c4 * 128, [[1024, 128], [1, 128]]),
                            r=wzb[:, c4 * DM + eh * 512:c4 * DM + eh * 512 + 512],
                            st=False, sp=(c4 == 3): e.matmul(o, l, r, start=st, stop=sp))
            zl_mark[sl] = mmz
            SC.wait(sT, mmz)
            em = SC.op(sS, lambda e, o=pst[:, sl * DM + eh * 512:sl * DM + (eh + 1) * 512],
                       s=pb[bank][:, 0:512]: e.copy(o, s))
            bank_guard[bank] = em
        SC.wait(sS, bank_guard[sl * 2 + 1])
        pm = SC.dma(sPS[sl], prt_in[th, lt:lt + 128, :], pst[:, sl * DM:(sl + 1) * DM])
        ps_mark[sl] = pm
        prt_marks.append(pm)
    p4_ps_tot = (sPS[0].n, sPS[1].n)
    sCCr = mksem("sCCr")
    GP.wait(sPS[0], p4_ps_tot[0])
    GP.wait(sPS[1], p4_ps_tot[1])
    GP.op(sCCr, lambda e: e.collective_compute(
        "ReduceScatter", mybir.AluOpType.add, replica_groups=PAIRS,
        ins=[prt_in[:]], outs=[prt_out[:]]))
    cc_prt = sCCr.n
    if "prt_out" in tap_out:
        SY.wait(sCCr, cc_prt)
        SY.dma(sTap, tap_out["prt_out"][:], prt_out[:])

    fence_all([(sV, sV.n), (sS, sS.n), (sT, sT.n),
               (sZL[0], sZL[0].n), (sZL[1], sZL[1].n),
               (sPS[0], sPS[0].n), (sPS[1], sPS[1].n), (sW4, sW4.n)])
    close_phase("p4")
    close_phase("py")

    MOE = dint("MOE", [TOK, DM])
    sW5 = mksem("sW5")
    wgs = psbuf("p5a", "wgs", [128, 8 * 8], F32)
    xnf = psbuf("p5a", "xnf", [128, 2 * 8 * 128], F32)
    gt = psbuf("p5a", "gt", [128, 16], F32)
    gsc = psbuf("p5a", "gsc", [128, 8], F32)
    GP.dma(sW5, ap3(wgs[:], 0, [[64, 128], [8, 8], [1, 8]]),
           ap3(wg, 0, [[8, 128], [128 * 8, 8], [1, 8]]))
    w5_tot = sW5.n
    sGL = mksems("sGL", 2)
    TE.wait(sW5, w5_tot)
    g_banks = {6: dl_banks[6], 7: dl_banks[7]}
    gl_mm = [0, 0]
    for t in range(32):
        sl = t % 2
        if t >= 2:
            SY.wait(sT, gl_mm[sl])
        gl = SY.dma(sGL[sl], ap3(xnf[:], sl * 1024, [[2 * 1024, 128], [128, 8], [1, 128]]),
                    ap3(XN0, 1 + t * 128, [[EXT, 128], [128 * EXT, 8], [1, 128]]))
        bank = 6 + sl
        TE.wait(sGL[sl], gl)
        TE.wait(sS, g_banks[bank])
        for ec in range(8):
            mm = TE.op(sT, lambda e, o=pb[bank][:, 0:8],
                       l=ap3(xnf[:], sl * 1024 + ec * 128, [[2 * 1024, 128], [1, 128]]),
                       r=wgs[:, ec * 8:(ec + 1) * 8],
                       st=(ec == 0), sp=(ec == 7): e.matmul(o, l, r, start=st, stop=sp))
        gl_mm[sl] = mm
        SC.wait(sT, mm)
        if t >= 1:
            SC.wait(sV, p5a_v_last)
        gm = SC.op(sS, lambda e, o=gt[:, 0:8], s=pb[bank][:, 0:8]:
                   e.activation(o, s, AF.Sigmoid))
        g_banks[bank] = gm
        VE.wait(sS, gm)
        VE.opw(sV, lambda e: e.reduce_max(gsc[:, 0:1], gt[:, 0:8], axis=AX))
        VE.opw(sV, lambda e: e.tensor_scalar(gt[:, 8:16], gt[:, 0:8], gsc[:, 0:1], None, ALU.is_equal))
        VE.opw(sV, lambda e: e.scalar_tensor_tensor(gsc[:, 4:5], gt[:, 8:9], 0.0, gt[:, 8:9], ALU.mult, ALU.add)) if False else None
        VE.opw(sV, lambda e: e.scalar_tensor_tensor(gt[:, 0:8], gt[:, 8:16], -1e30, gt[:, 0:8], ALU.mult, ALU.add))
        VE.opw(sV, lambda e: e.reduce_max(gsc[:, 1:2], gt[:, 0:8], axis=AX))
        VE.opw(sV, lambda e: e.tensor_scalar(gt[:, 0:8], gt[:, 0:8], gsc[:, 1:2], None, ALU.is_equal))
        VE.opw(sV, lambda e: e.tensor_tensor(gsc[:, 2:3], gsc[:, 0:1], gsc[:, 1:2], ALU.add))
        VE.opw(sV, lambda e: e.tensor_scalar(gsc[:, 2:3], gsc[:, 2:3], 3.0, None, ALU.mult))
        VE.opw(sV, lambda e: e.reciprocal(gsc[:, 3:4], gsc[:, 2:3]))
        VE.opw(sV, lambda e: e.tensor_scalar(gt[:, 8:16], gt[:, 8:16], gsc[:, 0:1], None, ALU.mult))
        VE.opw(sV, lambda e: e.scalar_tensor_tensor(gt[:, 8:16], gt[:, 0:8], gsc[:, 1:2], gt[:, 8:16], ALU.mult, ALU.add))
        p5a_v_last = VE.opw(sV, lambda e, o=wpTok[:, t * 8:(t + 1) * 8]:
               e.tensor_scalar(o, gt[:, 8:16], gsc[:, 3:4], None, ALU.mult))
    p5a_done = sV.n
    if "wp" in tap_out:
        SY.wait(sV, p5a_done)
        SY.dma(sTap, tap_out["wp"][:], wpTok[:])
    fence_all([(sV, sV.n), (sS, sS.n), (sT, sT.n),
               (sGL[0], sGL[0].n), (sGL[1], sGL[1].n), (sW5, w5_tot)])
    close_phase("p5a")

    xnb5 = psbuf("p5b", "xnb5", [128, 8 * 2048], BF16)
    h5 = psbuf("p5b", "h5", [128, 16 * 2048], BF16)
    acc5 = psbuf("p5b", "acc5", [128, 16 * DM], F32)
    w1sl = psbuf("p5b", "w1sl", [128, 2 * 1024], BF16)
    w3sl = psbuf("p5b", "w3sl", [128, 2 * 1024], BF16)
    wd5 = psbuf("p5b", "wd5", [128, 16 * 1024], BF16)
    s5 = psbuf("p5b", "s5", [128, 2 * 512], F32)
    sX5 = mksem("sX5")
    sWU = mksems("sWU", 2)
    sWDg = mksems("sWDg", 2)
    sMS = mksems("sMS", 2)
    exps = [(ws1[:], ws3[:], ws2[:], None)] + [(we1[ei], we3[ei], we2[ei], ei) for ei in range(NE)]
    moe_store = [0, 0]
    hm_prev = [0]
    silu_prev = [0]
    hm_sl = [0, 0]
    p5_bank0 = {0: bank_guard[0], 1: bank_guard[1], 2: bank_guard[2], 3: bank_guard[3],
                4: ag_banks[4], 5: ag_banks[5], 6: g_banks[6], 7: g_banks[7]}
    last_up_mm = 0
    wd_load = [0, 0]
    wd_free = [0, 0]
    for sp in range(2):
        if sp == 1:
            GP.wait(sT, last_up_mm)
        VE.wait(sMS[0], moe_store[0])
        VE.wait(sMS[1], moe_store[1])
        x5m = GP.dma(sX5, ap3(xnb5[:], 0, [[8 * 2048, 128], [2048, 8], [1, 2048]]),
                     ap3(XN0, 1 + sp * 2048, [[EXT, 128], [128 * EXT, 8], [1, 2048]]))
        TE.wait(sX5, x5m)
        if sp == 0:
            TE.wait(sV, p5a_done)
        wu_mark = [0, 0]
        for xi, (W1, W3, W2, ei) in enumerate(exps):
            ep = xi % 2
            up_last = 0
            for fs in range(16):
                slw = fs % 2
                GP.wait(sT, wu_mark[slw])
                l1 = GP.dma(sWU[slw], ap3(w1sl[:], slw * 1024, [[2 * 1024, 128], [128, 8], [1, 128]]),
                            ap3(W1, fs * 128, [[FF, 128], [128 * FF, 8], [1, 128]]))
                l3 = GP.dma(sWU[slw], ap3(w3sl[:], slw * 1024, [[2 * 1024, 128], [128, 8], [1, 128]]),
                            ap3(W3, fs * 128, [[FF, 128], [128 * FF, 8], [1, 128]]))
                if fs == 0:
                    GP.wait(sT, wd_free[0])
                w2m = GP.dma(sWDg[0], ap3(wd5[:], fs * 1024, [[16 * 1024, 128], [1, 1024]]),
                             ap3(W2, fs * 128 * DM, [[DM, 128], [1, 1024]]))
                TE.wait(sWU[slw], l3)
                mmu = 0
                if sp == 0 and xi == 0 and fs == 0:
                    for bank in range(8):
                        TE.wait(sS, p5_bank0[bank])
                    silu_tt = [0, 0, 0, 0]
                    hm_tt = [0, 0, 0, 0]
                hm = 0
                sm = 0
                for tt in range(4):
                    ts5 = tt % 2
                    TE.wait(sS, silu_tt[tt])
                    mm1 = 0
                    for ec in range(8):
                        mm1 = TE.op(sT, lambda e, o=pb[tt][:, 0:512],
                                    l=ap3(w1sl[:], slw * 1024 + ec * 128, [[2 * 1024, 128], [1, 128]]),
                                    r=ap3(xnb5[:], ec * 2048 + tt * 512, [[8 * 2048, 128], [1, 512]]),
                                    st=(ec == 0), sp=(ec == 7):
                                    e.matmul(o, l, r, start=st, stop=sp))
                    TE.wait(sV, hm_tt[tt])
                    mmu = 0
                    for ec in range(8):
                        mmu = TE.op(sT, lambda e, o=pb[4 + tt][:, 0:512],
                                    l=ap3(w3sl[:], slw * 1024 + ec * 128, [[2 * 1024, 128], [1, 128]]),
                                    r=ap3(xnb5[:], ec * 2048 + tt * 512, [[8 * 2048, 128], [1, 512]]),
                                    st=(ec == 0), sp=(ec == 7):
                                    e.matmul(o, l, r, start=st, stop=sp))
                    SC.wait(sT, mm1)
                    if hm_sl[ts5]:
                        SC.wait(sV, hm_sl[ts5])
                    sm = SC.op(sS, lambda e, o=s5[:, ts5 * 512:(ts5 + 1) * 512], s=pb[tt][:, 0:512]:
                               e.activation(o, s, AF.Silu))
                    silu_tt[tt] = sm
                    VE.wait(sS, sm)
                    VE.wait(sT, mmu)
                    hm = VE.op(sV, lambda e, o=h5[:, fs * 2048 + tt * 512:fs * 2048 + (tt + 1) * 512],
                               a=s5[:, ts5 * 512:(ts5 + 1) * 512], b_=pb[4 + tt][:, 0:512]:
                               e.tensor_tensor(o, a, b_, ALU.mult))
                    hm_tt[tt] = hm
                    hm_sl[ts5] = hm
                wu_mark[slw] = mmu
                last_up_mm = mmu
                up_last = mmu
                hm_prev[0] = hm
                silu_prev[0] = sm
            TE.wait(sV, hm_prev[0])
            TE.wait(sWDg[0], w2m)
            dn_last_v = 0
            mmd = 0
            for tt2 in range(16):
                b0, b1 = (0, 1) if tt2 % 2 == 0 else (2, 3)
                if tt2 >= 2:
                    TE.wait(sV, dn_marks[(tt2 - 2) % 2])
                for fc in range(16):
                    for eh in range(2):
                        mmd = TE.op(sT, lambda e, o=pb[(b0, b1)[eh]][:, 0:512],
                                    l=ap3(h5[:], fc * 2048 + tt2 * 128, [[16 * 2048, 128], [1, 128]]),
                                    r=ap3(wd5[:], fc * 1024 + eh * 512, [[16 * 1024, 128], [1, 512]]),
                                    st=(fc == 0), sp=(fc == 15):
                                    e.matmul(o, l, r, start=st, stop=sp))
                gt_i = (sp * 16 + tt2)
                VE.wait(sT, mmd)
                vm = 0
                for eh in range(2):
                    bank = (b0, b1)[eh]
                    dst5 = acc5[:, tt2 * DM + eh * 512:tt2 * DM + (eh + 1) * 512]
                    if ei is None:
                        vm = VE.op(sV, lambda e, o=dst5, s=pb[bank][:, 0:512]:
                                   e.tensor_scalar(o, s, 1.0 / 3.0, None, ALU.mult))
                    else:
                        vm = VE.op(sV, lambda e, o=dst5, s=pb[bank][:, 0:512],
                                   w_=wpTok[:, gt_i * 8 + ei:gt_i * 8 + ei + 1]:
                                   e.scalar_tensor_tensor(o, s, w_, o, ALU.mult, ALU.add))
                VE.wait(sV, vm)
                if tt2 == 0 and (sp == 0 and xi == 0):
                    dn_marks = [0, 0]
                dn_marks[tt2 % 2] = vm
            wd_free[0] = mmd
            TE.wait(sV, max(dn_marks))
        for tt2 in range(16):
            sls = tt2 % 2
            SC.wait(sV, dn_marks[tt2 % 2] if False else sV.n)
            SC.wait(sMS[sls], moe_store[sls])
            ms = SC.dma(sMS[sls], MOE[sp * 2048 + tt2 * 128:sp * 2048 + (tt2 + 1) * 128, :],
                        acc5[:, tt2 * DM:(tt2 + 1) * DM])
            moe_store[sls] = ms
    p5b_ms_tot = (sMS[0].n, sMS[1].n)

    fence_all([(sV, sV.n), (sS, sS.n), (sT, sT.n), (sX5, sX5.n),
               (sWU[0], sWU[0].n), (sWU[1], sWU[1].n), (sWDg[0], sWDg[0].n),
               (sMS[0], sMS[0].n), (sMS[1], sMS[1].n),
               (sGL[0], sGL[0].n), (sGL[1], sGL[1].n), (sW5, sW5.n),
               (sCCr, cc_prt)])
    close_phase("p5b")

    wrq = psbuf("p5c2", "wrq", [128, 8 * 128], F32)
    wrk = psbuf("p5c2", "wrk", [128, 8 * 128], F32)
    bqRb = psbuf("p5c2", "bqRb", [128, 128], F32)
    bkRb = psbuf("p5c2", "bkRb", [128, 128], F32)
    rts = psbuf("p5c", "rts", [64, 8], F32)
    wqTrow = psbuf("p5c", "wqTrow", [64, 2 * DM], F32)
    bqh_s = psbuf("p5c", "bqh_s", [64, 64], F32)
    brow = psbuf("p5c", "brow", [1, 256], F32)
    x0t5 = psbuf("p5c", "x0t5", [128, 2 * DM], F32)
    prtt = psbuf("p5c", "prtt", [128, 2 * DM], F32)
    moet = psbuf("p5c", "moet", [128, 2 * DM], F32)
    x1t5 = psbuf("p5c", "x1t5", [128, 2 * DM], F32)
    st5 = psbuf("p5c", "st5", [128, 16], F32)
    xnT5 = psbuf("p5c", "xnT5", [128, 2 * DM], F32)
    qr5 = psbuf("p5c", "qr5", [128, 128], F32)
    m45 = psbuf("p5c", "m45", [128, 64], F32)
    m1x = psbuf("p5c", "m1x", [128, 128], F32)
    oh5 = psbuf("p5c", "oh5", [128, 128], F32)
    ohst = psbuf("p5c", "ohst", [128, 2 * 128], BF16)
    sWR = mksem("sWR")
    sWQR = mksems("sWQR", 2)
    sL5 = mksems("sL5", 2)
    sS5 = mksems("sS5", 2)
    sA5 = mksems("sA5", 2)
    sO5 = mksems("sO5", 2)
    sBB = mksem("sBB")

    GP.dma(sWR, rts[:], RT[:])
    GP.dma(sWR, bqh_s[:, 0:16], bqhT[:])
    GP.dma(sWR, bqh_s[:, 16:32], bkhT[:])
    wr_tot = sWR.n
    wql_mark = [0, 0]
    for side, (wrdst, wT) in enumerate(((wrq, wqT_f), (wrk, wkT_f))):
        for hh in range(16):
            slh = hh % 2
            SY.wait(sT, wql_mark[slh])
            lw = SY.dma(sWQR[slh], wqTrow[:, slh * DM:(slh + 1) * DM], wT[hh * 64:hh * 64 + 64, :])
            TE.wait(sWQR[slh], lw)
            TE.wait(sWR, wr_tot)
            for ec in range(8):
                mmw = TE.op(sT, lambda e, o=pb[2][0:128, 0:8],
                            l=ap3(wqTrow[:], slh * DM + ec * 128, [[2 * DM, 64], [1, 128]]),
                            r=rts[:], st=True, sp=True: e.matmul(o, l, r, start=st, stop=sp))
                wql_mark[slh] = mmw
                SC.wait(sT, mmw)
                wr_ev = SC.op(sS, lambda e, o=wrdst[:, ec * 128 + hh * 8:ec * 128 + hh * 8 + 8],
                              s=pb[2][0:128, 0:8]: e.copy(o, s))
                TE.wait(sS, wr_ev)
    TE.wait(sWR, wr_tot)
    for side in range(2):
        mmb = TE.op(sT, lambda e, o=pb[2][0:16, 0:8],
                    l=bqh_s[:, side * 16:(side + 1) * 16],
                    r=rts[:], st=True, sp=True: e.matmul(o, l, r, start=st, stop=sp))
        SC.wait(sT, mmb)
        ev = SC.op(sS, lambda e, o=bqh_s[0:16, 32 + side * 16:32 + side * 16 + 8],
                   s=pb[2][0:16, 0:8]: e.copy(o, s))
        TE.wait(sS, ev)
        GP.wait(sS, ev)
        m1_ = GP.dma(sBB, ap3(brow[:], side * 128, [[256, 1], [8, 16], [1, 8]]),
                     ap3(bqh_s[:], 32 + side * 16, [[64, 16], [1, 8]]))
        GP.wait(sBB, m1_)
        GP.dma(sBB, ap3((bqRb, bkRb)[side][:], 0, [[128, 128], [0, 1], [1, 128]]),
               ap3(brow[:], side * 128, [[256, 1], [0, 128], [1, 128]]))
    bb_tot = sBB.n
    VE.wait(sBB, bb_tot)

    x1s_marks = []
    ag_st = [0, 0]
    oh_st = [0, 0]
    tr_banks5 = {0: 0, 1: 0}
    qrk_bank = {3: 0}
    l5_marks = [0, 0]
    for t in range(32):
        sl = t % 2
        bb = sl * DM
        if t >= 2:
            SY.wait(sV, x1s_marks[t - 2])
        l1 = SY.dma(sL5[sl], x0t5[:, bb:bb + DM], x_ext[1 + t * 128:1 + (t + 1) * 128, :])
        l2 = SY.dma(sL5[sl], prtt[:, bb:bb + DM], prt_out[t * 128:(t + 1) * 128, :])
        l3 = SY.dma(sL5[sl], moet[:, bb:bb + DM], MOE[t * 128:(t + 1) * 128, :])
        VE.wait(sL5[sl], l3)
        VE.opw(sV, lambda e, o=x1t5[:, bb:bb + DM], a=x0t5[:, bb:bb + DM], b_=prtt[:, bb:bb + DM]:
               e.tensor_tensor(o, a, b_, ALU.add))
        xm1 = VE.opw(sV, lambda e, o=x1t5[:, bb:bb + DM], b_=moet[:, bb:bb + DM]:
                     e.tensor_tensor(o, o, b_, ALU.add))
        SC.wait(sV, xm1)
        sx1 = SC.dma(sS5[sl], X1[t * 128:(t + 1) * 128, :], x1t5[:, bb:bb + DM])
        VE.add(lambda e, a=st5[:, 0:1], b_=x1t5[:, bb:bb + DM]: e.reduce_sum(a, b_, axis=AX))
        VE.opw(sV, lambda e, a=moet[:, bb:bb + DM], b_=x1t5[:, bb:bb + DM]:
               e.tensor_tensor(a, b_, b_, ALU.mult))
        VE.opw(sV, lambda e, a=st5[:, 1:2], b_=moet[:, bb:bb + DM]: e.reduce_sum(a, b_, axis=AX))
        VE.add(lambda e: e.tensor_scalar(st5[:, 2:3], st5[:, 0:1], 1.0 / DM, None, ALU.mult))
        VE.opw(sV, lambda e: e.tensor_scalar(st5[:, 3:4], st5[:, 1:2], 1.0 / DM, None, ALU.mult))
        VE.opw(sV, lambda e: e.tensor_tensor(st5[:, 4:5], st5[:, 2:3], st5[:, 2:3], ALU.mult))
        VE.opw(sV, lambda e: e.tensor_tensor(st5[:, 5:6], st5[:, 3:4], st5[:, 4:5], ALU.subtract))
        var5 = VE.op(sV, lambda e: e.tensor_scalar(st5[:, 5:6], st5[:, 5:6], 1e-5, None, ALU.add))
        SC.wait(sV, var5)
        sq5 = SC.op(sS, lambda e: e.sqrt(st5[:, 6:7], st5[:, 5:6]))
        VE.wait(sS, sq5)
        rc5 = VE.opw(sV, lambda e: e.reciprocal(st5[:, 7:8], st5[:, 6:7]))
        VE.wait(sS5[sl], sx1)
        nm5 = VE.op(sV, lambda e, o=x1t5[:, bb:bb + DM], x_=x1t5[:, bb:bb + DM]:
                    e.tensor_scalar(o, x_, st5[:, 2:3], st5[:, 7:8], ALU.subtract, ALU.mult))
        x1s_marks.append(nm5)
        TE.wait(sV, nm5)
        for c in range(8):
            bank = pb[c % 2]
            if t > 0 or c >= 2:
                TE.wait(sS, tr_banks5[c % 2])
            tmk = TE.op(sT, lambda e, o=bank[:, 0:128], s=x1t5[:, bb + c * 128:bb + (c + 1) * 128]:
                        e.transpose(o, s, identf[:]))
            SC.wait(sT, tmk)
            if t >= 2 and c < 2:
                SC.wait(sA5[sl], ag_st[sl])
            evk = SC.op(sS, lambda e, o=xnT5[:, bb + c * 128:bb + (c + 1) * 128], s=bank[:, 0:128]:
                        e.copy(o, s))
            tr_banks5[c % 2] = evk
        GP.wait(sS, tr_banks5[(8 - 1) % 2])
        GP.wait(sS, tr_banks5[0 if (8 - 1) % 2 == 1 else 1])
        agm = GP.dma(sA5[sl], ap3(agx_in, t * 128, [[TOK, 128], [128 * TOK, 8], [1, 128]]),
                     ap3(xnT5[:], bb, [[2 * DM, 128], [128, 8], [1, 128]]))
        ag_st[sl] = agm
        TE.wait(sS, tr_banks5[0])
        TE.wait(sS, tr_banks5[1])
        for side, (wrs, brb, rowoff) in enumerate(((wrq, bqRb, 1024), (wrk, bkRb, 1152))):
            TE.wait(sV, qrk_bank[3])
            for ec in range(8):
                qmm = TE.op(sT, lambda e, o=pb[3][:, 0:128],
                            l=ap3(xnT5[:], bb + ec * 128, [[2 * DM, 128], [1, 128]]),
                            r=wrs[:, ec * 128:(ec + 1) * 128],
                            st=(ec == 0), sp=(ec == 7): e.matmul(o, l, r, start=st, stop=sp))
            VE.wait(sT, qmm)
            VE.opw(sV, lambda e, b_=brb[:]: e.tensor_tensor(qr5[:], pb[3][:, 0:128], b_, ALU.add))
            qrk_bank[3] = sV.n
            VE.opw(sV, lambda e: e.tensor_tensor(
                ap3(m45[:], 0, [[64, 128], [4, 16], [1, 4]]),
                ap3(qr5[:], 0, [[128, 128], [8, 16], [1, 4]]),
                ap3(qr5[:], 4, [[128, 128], [8, 16], [1, 4]]), ALU.max))
            VE.opw(sV, lambda e: e.tensor_tensor(
                ap3(m45[:], 32, [[64, 128], [2, 16], [1, 2]]),
                ap3(m45[:], 0, [[64, 128], [4, 16], [1, 2]]),
                ap3(m45[:], 2, [[64, 128], [4, 16], [1, 2]]), ALU.max))
            VE.opw(sV, lambda e: e.tensor_tensor(
                ap3(m45[:], 48, [[64, 128], [1, 16], [1, 1]]),
                ap3(m45[:], 32, [[64, 128], [2, 16], [1, 1]]),
                ap3(m45[:], 33, [[64, 128], [2, 16], [1, 1]]), ALU.max))
            VE.opw(sV, lambda e: e.tensor_copy(
                ap3(m1x[:], 0, [[128, 128], [8, 16], [1, 1]]),
                ap3(m45[:], 48, [[64, 128], [1, 16], [1, 1]])))
            VE.opw(sV, lambda e: e.tensor_copy(
                ap3(m1x[:], 1, [[128, 128], [8, 16], [1, 1]]),
                ap3(m1x[:], 0, [[128, 128], [8, 16], [1, 1]])))
            VE.opw(sV, lambda e: e.tensor_copy(
                ap3(m1x[:], 2, [[128, 128], [8, 16], [1, 2]]),
                ap3(m1x[:], 0, [[128, 128], [8, 16], [1, 2]])))
            VE.opw(sV, lambda e: e.tensor_copy(
                ap3(m1x[:], 4, [[128, 128], [8, 16], [1, 4]]),
                ap3(m1x[:], 0, [[128, 128], [8, 16], [1, 4]])))
            ohm = VE.opw(sV, lambda e: e.tensor_tensor(oh5[:], qr5[:], m1x[:], ALU.is_equal))
            TE.wait(sV, ohm)
            TE.wait(sS, tr_banks5[0])
            tmo = TE.op(sT, lambda e, o=pb[0][:, 0:128], s=oh5[:]: e.transpose(o, s, identf[:]))
            SC.wait(sT, tmo)
            SC.wait(sO5[side], oh_st[side])
            sc_ = 30000.0 if side == 0 else 1.0
            eco = SC.op(sS, lambda e, o=ohst[:, side * 128:side * 128 + 128], s=pb[0][:, 0:128],
                        k=sc_: e.activation(o, s, AF.Copy, scale=k))
            tr_banks5[0] = eco
            SC.wait(sS, eco)
            som = SC.dma(sO5[side], ap3(agx_in, rowoff * TOK + t * 128, [[TOK, 128], [1, 128]]),
                         ohst[:, side * 128:side * 128 + 128])
            oh_st[side] = som
    p5c_sx_tot = (sS5[0].n, sS5[1].n)
    p5c_ag_tot = (sA5[0].n, sA5[1].n)
    p5c_oh_tot = (sO5[0].n, sO5[1].n)
    if "x1" in tap_out:
        SY.wait(sS5[0], p5c_sx_tot[0])
        SY.wait(sS5[1], p5c_sx_tot[1])
        SY.dma(sTap, tap_out["x1"][:], X1[:])

    sCC6 = mksem("sCC6")
    GP.wait(sA5[0], p5c_ag_tot[0])
    GP.wait(sA5[1], p5c_ag_tot[1])
    GP.wait(sO5[0], p5c_oh_tot[0])
    GP.wait(sO5[1], p5c_oh_tot[1])
    GP.op(sCC6, lambda e: e.collective_compute(
        "AllGather", mybir.AluOpType.bypass, replica_groups=PAIRS,
        ins=[agx_in[:]], outs=[agx_out[:]]))
    cc_agx = sCC6.n
    if "agx_out" in tap_out:
        SY.wait(sCC6, cc_agx)
        SY.dma(sTap, tap_out["agx_out"][:], agx_out[:])

    fence_all([(sV, sV.n), (sS, sS.n), (sT, sT.n), (sCC6, cc_agx),
               (sL5[0], sL5[0].n), (sL5[1], sL5[1].n),
               (sS5[0], p5c_sx_tot[0]), (sS5[1], p5c_sx_tot[1]),
               (sWQR[0], sWQR[0].n), (sWQR[1], sWQR[1].n), (sBB, bb_tot)])
    close_phase("p5c")

    sW6 = mksem("sW6")
    wqb = psbuf("p6", "wqb", [128, 8 * 512], BF16)
    wkb = psbuf("p6", "wkb", [128, 8 * 512], BF16)
    wvb = psbuf("p6", "wvb", [128, 8 * 512], BF16)
    wob = psbuf("p6", "wob", [128, 4 * DM], BF16)
    bqh8 = psbuf("p6", "bqh8", [64, 32], F32)
    bvrow = psbuf("p6", "bvrow", [1, 512], F32)
    bvb = psbuf("p6", "bvb", [128, 512], F32)
    QT6 = psbuf("p6", "QT6", [128, SEQ], BF16)
    KT6 = psbuf("p6", "KT6", [128, SEQ], BF16)
    vp6 = psbuf("p6", "vp6", [128, 64 * 65], BF16)
    aot = psbuf("p6", "aot", [128, 4 * SEQ], BF16)
    xt6 = psbuf("p6", "xt6", [128, 2 * 8 * 512], BF16)
    PT6 = psbuf("p6", "PT6", [128, 2 * 1024], BF16)
    rd6 = psbuf("p6", "rd6", [1, 2 * 512], F32)
    cm30 = psbuf("p6", "cm30", [128, 1], F32)
    rdb6 = psbuf("p6", "rdb6", [64, 2 * 512], F32)
    art_st = psbuf("p6", "art_st", [128, 2 * DM], F32)
    for ec in range(8):
        GP.dma(sW6, wqb[:, ec * 512:(ec + 1) * 512], wq_my[ec * 128:(ec + 1) * 128, :])
        GP.dma(sW6, wkb[:, ec * 512:(ec + 1) * 512], wk_my[ec * 128:(ec + 1) * 128, :])
        GP.dma(sW6, wvb[:, ec * 512:(ec + 1) * 512], wv_my[ec * 128:(ec + 1) * 128, :])
    for c4 in range(4):
        GP.dma(sW6, wob[:, c4 * DM:(c4 + 1) * DM], wo_my[c4 * 128:(c4 + 1) * 128, :])
    GP.dma(sW6, bqh8[:, 0:8], bqh_my[:])
    GP.dma(sW6, bqh8[:, 8:16], bkh_my[:])
    bvl = GP.dma(sW6, bvrow[0:1, :], bv_my[None, :])
    GP.add(lambda e, v=bvl: e.wait_ge(sW6.h, v))
    GP.dma(sW6, ap3(bvb[:], 0, [[512, 128], [0, 1], [1, 512]]),
           ap3(bvrow[:], 0, [[512, 1], [0, 128], [1, 512]]))
    w6_tot = sW6.n
    VE.wait(sW6, w6_tot)
    bq8m = VE.opw(sV, lambda e: e.tensor_scalar(bqh8[0:64, 16:24], bqh8[0:64, 0:8], 0.125, None, ALU.mult))
    on_m = VE.opw(sV, lambda e: e.memset(ap3(vp6[:], 64, [[64 * 65, 128], [65, 64], [1, 1]]), 1.0))
    cm_m = VE.opw(sV, lambda e: e.memset(cm30[:], -30016.0))

    sA6 = mksems("sA6", 2)
    sOH6 = mksem("sOH6")
    sRB6 = mksems("sRB6", 2)
    sAR = mksems("sAR", 2)
    qk_banks = {0: 0, 1: 0, 2: 0, 3: 0}
    sc_banks = {66: 0, 67: 0}
    v_banks = {3: 0}
    pv_bank6 = {0: 0, 3: 0}
    rb_slot = [0, 0]
    an_slot = [0, 0]
    pt_con = [0, 0]
    a6_mm = [0, 0]
    TE.wait(sW6, w6_tot)
    TE.wait(sV, on_m)
    TE.wait(sV, cm_m)
    TE.wait(sV, bq8m)
    SC.wait(sV, cm_m)
    art_marks = [0, 0]
    aot_done = 0
    last_score_mm = 0
    for h in range(8):
        if h > 0:
            GP.wait(sT, last_score_mm)
        for th in range(2):
            GP.dma(sOH6, QT6[64:72, th * TOK:(th + 1) * TOK],
                   agx_out[th * 1280 + 1024 + h * 8:th * 1280 + 1024 + h * 8 + 8, :])
            GP.dma(sOH6, KT6[64:72, th * TOK:(th + 1) * TOK],
                   agx_out[th * 1280 + 1152 + h * 8:th * 1280 + 1152 + h * 8 + 8, :])
        oh_tot = sOH6.n
        for tq in range(16):
            sl6 = tq % 2
            if h > 0 or tq >= 2:
                SY.wait(sT, a6_mm[sl6])
            th = tq // 8
            la = SY.dma(sA6[sl6], ap3(xt6[:], sl6 * 4096, [[2 * 8 * 512, 128], [512, 8], [1, 512]]),
                        ap3(agx_out, th * 1280 * TOK + (tq % 8) * 512, [[TOK, 128], [128 * TOK, 8], [1, 512]]))
            TE.wait(sA6[sl6], la)
            bk_ = 1
            TE.wait(sV, qk_banks[bk_])
            for ec in range(8):
                mk = TE.op(sT, lambda e, o=pb[bk_][0:64, 0:512],
                           l=wkb[:, ec * 512 + h * 64:ec * 512 + h * 64 + 64],
                           r=ap3(xt6[:], sl6 * 4096 + ec * 512, [[2 * 8 * 512, 128], [1, 512]]),
                           st=(ec == 0), sp=(ec == 7): e.matmul(o, l, r, start=st, stop=sp))
            bq_ = 2
            TE.wait(sV, qk_banks[bq_])
            for ec in range(8):
                mq = TE.op(sT, lambda e, o=pb[bq_][0:64, 0:512],
                           l=wqb[:, ec * 512 + h * 64:ec * 512 + h * 64 + 64],
                           r=ap3(xt6[:], sl6 * 4096 + ec * 512, [[2 * 8 * 512, 128], [1, 512]]),
                           st=(ec == 0), sp=(ec == 7): e.matmul(o, l, r, start=st, stop=sp))
            a6_mm[sl6] = mq
            VE.wait(sT, mk)
            ek = VE.op(sV, lambda e, o=ap3(KT6[:], tq * 512, [[SEQ, 64], [1, 512]]),
                       s=pb[bk_][0:64, 0:512], b_=bqh8[0:64, 8 + h:9 + h]:
                       e.tensor_scalar(o, s, b_, None, ALU.add))
            qk_banks[bk_] = ek
            VE.wait(sT, mq)
            eq = VE.op(sV, lambda e, o=ap3(QT6[:], tq * 512, [[SEQ, 64], [1, 512]]),
                       s=pb[bq_][0:64, 0:512], b_=bqh8[0:64, 16 + h:17 + h]:
                       e.tensor_scalar(o, s, 0.125, b_, ALU.mult, ALU.add))
            qk_banks[bq_] = eq
            for ts4 in range(4):
                vb_ = 3
                TE.wait(sV, v_banks[vb_])
                TE.wait(sV, pv_bank6[3])
                for ec in range(8):
                    mv = TE.op(sT, lambda e, o=pb[vb_][:, 0:64],
                               l=ap3(xt6[:], sl6 * 4096 + ec * 512 + ts4 * 128, [[2 * 8 * 512, 128], [1, 128]]),
                               r=wvb[:, ec * 512 + h * 64:ec * 512 + h * 64 + 64],
                               st=(ec == 0), sp=(ec == 7): e.matmul(o, l, r, start=st, stop=sp))
                a6_mm[sl6] = mv
                VE.wait(sT, mv)
                ev = VE.opw(sV, lambda e, o=ap3(vp6[:], (tq * 4 + ts4) * 65, [[64 * 65, 128], [1, 64]]),
                            s=pb[vb_][:, 0:64], b_=bvb[:, h * 64:h * 64 + 64]:
                            e.tensor_tensor(o, s, b_, ALU.add))
                v_banks[vb_] = ev
        TE.wait(sOH6, oh_tot)
        TE.wait(sV, v_banks[3])
        for qc in range(16):
            pvi = 0 if qc % 2 == 0 else 3
            pvb = pb[pvi]
            if h == 0 and qc == 0:
                TE.wait(sS, g_banks[6])
                TE.wait(sS, g_banks[7])
            TE.wait(sV, pv_bank6[pvi])
            exp_m = [0, 0]
            for j in range(32):
                pj = j % 2
                ps_ = psA if pj == 0 else psB
                TE.wait(sS, sc_banks[66 + pj])
                for half in range(2):
                    kt = 2 * j + half
                    ms = TE.op(sT, lambda e, o=ps_[:, half * 512:(half + 1) * 512],
                               l=ap3(KT6[:], kt * 128, [[SEQ, 72], [1, 128]]),
                               r=ap3(QT6[:], qc * 512, [[SEQ, 72], [1, 512]]),
                               st=True, sp=True: e.matmul(o, l, r, start=st, stop=sp))
                last_score_mm = ms
                if j > 0:
                    TE.wait(sS, exp_m[(j - 1) % 2])
                    for half in range(2):
                        kt = 2 * (j - 1) + half
                        mpv = TE.op(sT, lambda e, o=pvb[0:65, 0:512],
                                    l=ap3(vp6[:], kt * 65, [[64 * 65, 128], [1, 65]]),
                                    r=PT6[:, ((j - 1) % 2) * 1024 + half * 512:
                                            ((j - 1) % 2) * 1024 + (half + 1) * 512],
                                    st=(kt == 0), sp=False: e.matmul(o, l, r, start=st, stop=sp))
                    pt_con[(j - 1) % 2] = mpv
                SC.wait(sT, ms)
                if pt_con[pj]:
                    SC.wait(sT, pt_con[pj])
                exp_m[pj] = SC.op(sS, lambda e, o=PT6[:, pj * 1024:(pj + 1) * 1024],
                                  s=ps_[:, 0:1024]:
                                  e.activation(o, s, AF.Exp, bias=cm30[:, 0:1]))
                sc_banks[66 + pj] = exp_m[pj]
            TE.wait(sS, exp_m[31 % 2])
            for half in range(2):
                kt = 62 + half
                mpv = TE.op(sT, lambda e, o=pvb[0:65, 0:512],
                            l=ap3(vp6[:], kt * 65, [[64 * 65, 128], [1, 65]]),
                            r=PT6[:, (31 % 2) * 1024 + half * 512:(31 % 2) * 1024 + (half + 1) * 512],
                            st=False, sp=(kt == 63): e.matmul(o, l, r, start=st, stop=sp))
            pt_con[31 % 2] = mpv
            slr = qc % 2
            VE.wait(sT, mpv)
            VE.wait(sRB6[slr], rb_slot[slr])
            rc6 = VE.opw(sV, lambda e, o=rd6[0:1, slr * 512:slr * 512 + 512],
                         s=pvb[64:65, 0:512]: e.reciprocal(o, s))
            GP.wait(sV, rc6)
            GP.wait(sV, an_slot[slr])
            rb = GP.dma(sRB6[slr], ap3(rdb6[:], slr * 512, [[1024, 64], [0, 1], [1, 512]]),
                        ap3(rd6[:], slr * 512, [[1024, 1], [0, 64], [1, 512]]))
            rb_slot[slr] = rb
            VE.wait(sRB6[slr], rb)
            po = (h % 2) * 64
            an = VE.opw(sV, lambda e,
                        o=aot[po:po + 64, (h // 2) * SEQ + qc * 512:(h // 2) * SEQ + (qc + 1) * 512],
                        s=pvb[0:64, 0:512], r_=rdb6[:, slr * 512:slr * 512 + 512]:
                        e.tensor_tensor(o, s, r_, ALU.mult))
            an_slot[slr] = an
            pv_bank6[pvi] = an
        aot_done = sV.n

    TE.wait(sV, aot_done)
    TE.wait(sV, qk_banks[0])
    TE.wait(sV, qk_banks[1])
    TE.wait(sV, qk_banks[2])
    TE.wait(sV, qk_banks[3])
    ar_banks = {0: 0, 1: 0, 2: 0, 3: 0}
    ar_ps = [0, 0]
    for t in range(64):
        th, lt = t // 32, (t % 32) * 128
        sl = t % 2
        SC.wait(sAR[sl], ar_ps[sl])
        mwo = 0
        for eh in range(2):
            bank = sl * 2 + eh
            TE.wait(sS, ar_banks[bank])
            for cp in range(4):
                mwo = TE.op(sT, lambda e, o=pb[bank][:, 0:512],
                            l=ap3(aot[:], cp * SEQ + t * 128, [[4 * SEQ, 128], [1, 128]]),
                            r=wob[:, cp * DM + eh * 512:cp * DM + eh * 512 + 512],
                            st=(cp == 0), sp=(cp == 3): e.matmul(o, l, r, start=st, stop=sp))
            SC.wait(sT, mwo)
            ea = SC.op(sS, lambda e, o=art_st[:, sl * DM + eh * 512:sl * DM + (eh + 1) * 512],
                       s=pb[bank][:, 0:512]: e.copy(o, s))
            ar_banks[bank] = ea
        SC.wait(sS, ar_banks[sl * 2 + 1])
        pa = SC.dma(sAR[sl], art_in[th, lt:lt + 128, :], art_st[:, sl * DM:(sl + 1) * DM])
        ar_ps[sl] = pa
    p6_ar_tot = (sAR[0].n, sAR[1].n)
    sCC7 = mksem("sCC7")
    GP.wait(sAR[0], p6_ar_tot[0])
    GP.wait(sAR[1], p6_ar_tot[1])
    GP.op(sCC7, lambda e: e.collective_compute(
        "ReduceScatter", mybir.AluOpType.add, replica_groups=PAIRS,
        ins=[art_in[:]], outs=[art_out[:]]))
    cc_art = sCC7.n
    if "art_out" in tap_out:
        SY.wait(sCC7, cc_art)
        SY.dma(sTap, tap_out["art_out"][:], art_out[:])

    fence_all([(sV, sV.n), (sS, sS.n), (sT, sT.n), (sW6, w6_tot),
               (sA6[0], sA6[0].n), (sA6[1], sA6[1].n), (sOH6, sOH6.n),
               (sRB6[0], sRB6[0].n), (sRB6[1], sRB6[1].n),
               (sAR[0], p6_ar_tot[0]), (sAR[1], p6_ar_tot[1])])
    close_phase("p6")

    sL7 = mksems("sL7", 6)
    sO7 = mksems("sO7", 2)
    sWF = mksems("sWF", 2)
    sW7 = mksem("sW7")
    sgn_src = psbuf("p7", "sgn_src", [128, 2 * DM], F32)
    sgnb = psbuf("p7", "sgnb", [128, 8 * DM], BF16)
    borow = psbuf("p7", "borow", [1, DM], F32)
    bob = psbuf("p7", "bob", [128, DM], F32)
    x17 = psbuf("p7", "x17", [128, 2 * DM], F32)
    art7 = psbuf("p7", "art7", [128, 2 * DM], F32)
    xnl7 = psbuf("p7", "xnl7", [128, 2 * 8 * 128], BF16)
    x27 = psbuf("p7", "x27", [128, 2 * DM], F32)
    bol = GP.dma(sW7, borow[0:1, :], bo[None, :])
    GP.add(lambda e, v=bol: e.wait_ge(sW7.h, v))
    GP.dma(sW7, ap3(bob[:], 0, [[DM, 128], [0, 1], [1, DM]]),
           ap3(borow[:], 0, [[DM, 1], [0, 128], [1, DM]]))
    w7_tot = sW7.n
    sgf_marks = [0, 0]
    for ec in range(8):
        slf = ec % 2
        if ec >= 2:
            SY.wait(sS, sgf_marks[slf])
        lf = SY.dma(sWF[slf], sgn_src[:, slf * DM:(slf + 1) * DM], w_ffnT[ec * 128:(ec + 1) * 128, :])
        SC.wait(sWF[slf], lf)
        sg_m = SC.op(sS, lambda e, o=sgnb[:, ec * DM:(ec + 1) * DM], s=sgn_src[:, slf * DM:(slf + 1) * DM]:
                     e.sign(o, s))
        sgf_marks[slf] = sg_m
    sgn_done = sS.n
    VE.wait(sW7, w7_tot)
    TE.wait(sS, sS.n)
    TE.wait(sV, sV.n)
    f7_banks = {4: 0, 5: 0, 6: 0, 7: 0}
    SY.wait(sCC7, cc_art)
    o7_marks = [0, 0]
    l7_con = [0, 0]
    for t in range(32):
        sl = t % 2
        bb = sl * DM
        if t >= 2:
            SY.wait(sV, l7_con[sl])
        la = SY.dma(sL7[sl], x17[:, bb:bb + DM], X1[t * 128:(t + 1) * 128, :])
        lb = SY.dma(sL7[2 + sl], art7[:, bb:bb + DM], art_out[t * 128:(t + 1) * 128, :])
        lc = SY.dma(sL7[4 + sl], ap3(xnl7[:], sl * 1024, [[2 * 1024, 128], [128, 8], [1, 128]]),
                    ap3(agx_in, t * 128, [[TOK, 128], [128 * TOK, 8], [1, 128]]))
        TE.wait(sL7[4 + sl], lc)
        if t == 0:
            TE.wait(sS, sgn_done)
        m7 = 0
        for eh in range(2):
            bnk = (4 + sl) if eh == 0 else (6 + sl)
            TE.wait(sV, f7_banks[bnk])
            for ec in range(8):
                m7 = TE.op(sT, lambda e, o=pb[bnk][:, 0:512],
                           l=ap3(xnl7[:], sl * 1024 + ec * 128, [[2 * 1024, 128], [1, 128]]),
                           r=ap3(sgnb[:], ec * DM + eh * 512, [[8 * DM, 128], [1, 512]]),
                           st=(ec == 0), sp=(ec == 7): e.matmul(o, l, r, start=st, stop=sp))
        VE.wait(sL7[sl], la)
        VE.wait(sL7[2 + sl], lb)
        VE.wait(sO7[sl], o7_marks[sl])
        VE.wait(sT, m7)
        VE.opw(sV, lambda e, o=x27[:, bb:bb + DM], a=x17[:, bb:bb + DM], b_=art7[:, bb:bb + DM]:
               e.tensor_tensor(o, a, b_, ALU.add))
        VE.opw(sV, lambda e, o=x27[:, bb:bb + DM]: e.tensor_tensor(o, o, bob[:], ALU.add))
        for eh in range(2):
            bnk = (4 + sl) if eh == 0 else (6 + sl)
            vm7 = VE.opw(sV, lambda e, o=x27[:, bb + eh * 512:bb + (eh + 1) * 512],
                         s=pb[bnk][:, 0:512]: e.tensor_tensor(o, o, s, ALU.add))
            f7_banks[bnk] = vm7
        l7_con[sl] = vm7
        SC.wait(sV, vm7)
        SC.wait(sO7[sl], o7_marks[sl])
        om = SC.dma(sO7[sl], out_my[t * 128:(t + 1) * 128, :], x27[:, bb:bb + DM])
        o7_marks[sl] = om
    out_tot = (sO7[0].n, sO7[1].n)


    end_sems = [sConst, sW0, sPC, sXL[0], sXL[1], sXS[0], sXS[1], sTap, sTapB]
    for s_ in end_sems:
        SY.wait(s_, s_.n)

    for ph in reversed(list(ph_stacks)):
        close_phase(ph)

    with nc.Block() as block:
        block.sync(lambda e: [f(e) for f in SY.ops])
        block.scalar(lambda e: [f(e) for f in SC.ops])
        block.vector(lambda e: [f(e) for f in VE.ops])
        block.tensor(lambda e: [f(e) for f in TE.ops])
        block.gpsimd(lambda e: [f(e) for f in GP.ops])

    stack.close()
    return nc, tap_out


def prepare_inputs(inputs):
    x = np.ascontiguousarray(inputs["L0_attn_x"])
    f = np.float32
    w_inT_ = np.ascontiguousarray(inputs["L0_attn_W_in"].T)
    kcat_ = np.ascontiguousarray(
        np.concatenate([inputs["L0_attn_Kx"][:, 0, :], inputs["L0_attn_Kz"][:, 0, :]], 0))
    w_xprojT_ = np.ascontiguousarray(inputs["L0_attn_W_xproj"].T)
    w_dtT = np.ascontiguousarray(inputs["L0_attn_W_dt"].T)
    woutT = np.ascontiguousarray(inputs["L0_attn_W_out"].T)
    ident = np.eye(128, dtype=f)
    import ml_dtypes
    identb_ = ident.astype(ml_dtypes.bfloat16)

    maps = []
    for k in range(8):
        b, pp = k // 2, k % 2
        x_ext_ = np.zeros((EXT, DM), f)
        x_ext_[1:TOK + 1] = x[b, pp * TOK:(pp + 1) * TOK]
        if pp == 1:
            x_ext_[0] = x[b, TOK - 1]
        if pp == 0:
            x_ext_[TOK + 1] = x[b, TOK]
        c0 = pp * 256
        A_sl = np.asarray(inputs["L0_attn_A_log"][c0:c0 + 256], f)
        A_arr_ = np.zeros((128, 16), f)
        D_arr_ = np.zeros((128, 2), f)
        for ch in range(2):
            A_arr_[:, ch * 8:(ch + 1) * 8] = A_sl[ch * 128:(ch + 1) * 128]
            D_arr_[:, ch] = np.asarray(inputs["L0_attn_D"][c0 + ch * 128:c0 + (ch + 1) * 128], f)
        hs = pp * 512
        m = {
            "x_ext": x_ext_,
            "w_inT": w_inT_,
            "kcat": kcat_,
            "w_xprojT": w_xprojT_,
            "w_dtT_my": np.ascontiguousarray(w_dtT[:, c0:c0 + 256]),
            "b_dt_my": np.ascontiguousarray(np.asarray(inputs["L0_attn_b_dt"], f)[c0:c0 + 256]),
            "b_dt_arr": np.ascontiguousarray(
                np.asarray(inputs["L0_attn_b_dt"], f)[c0:c0 + 256].reshape(2, 128).T),
            "A_arr": A_arr_,
            "D_arr": D_arr_,
            "woutT_y": np.ascontiguousarray(woutT[c0:c0 + 256]),
            "woutT_z": np.ascontiguousarray(woutT[512:1024]),
            "wg": np.asarray(inputs["L0_ffn_Wg"], f),
            "ws1": np.asarray(inputs["L0_ffn_Ws1"], f),
            "ws3": np.asarray(inputs["L0_ffn_Ws3"], f),
            "ws2": np.asarray(inputs["L0_ffn_Ws2"], f),
            "we1": np.asarray(inputs["L0_ffn_We1"], f),
            "we3": np.asarray(inputs["L0_ffn_We3"], f),
            "we2": np.asarray(inputs["L0_ffn_We2"], f),
            "wq_f": np.asarray(inputs["L1_attn_Wq"], f),
            "wk_f": np.asarray(inputs["L1_attn_Wk"], f),
            "RT": np.ascontiguousarray(np.asarray(inputs["L1_attn_R"], f).T),
            "bqhT": np.ascontiguousarray(np.asarray(inputs["L1_attn_bq"], f).reshape(16, 64).T),
            "bkhT": np.ascontiguousarray(np.asarray(inputs["L1_attn_bk"], f).reshape(16, 64).T),
            "wqT_f": np.ascontiguousarray(np.asarray(inputs["L1_attn_Wq"], f).T),
            "wkT_f": np.ascontiguousarray(np.asarray(inputs["L1_attn_Wk"], f).T),
            "bqh_my": np.ascontiguousarray(
                np.asarray(inputs["L1_attn_bq"], f)[hs:hs + 512].reshape(8, 64).T),
            "bkh_my": np.ascontiguousarray(
                np.asarray(inputs["L1_attn_bk"], f)[hs:hs + 512].reshape(8, 64).T),
            "wq_my": np.ascontiguousarray(np.asarray(inputs["L1_attn_Wq"], f)[:, hs:hs + 512]),
            "wk_my": np.ascontiguousarray(np.asarray(inputs["L1_attn_Wk"], f)[:, hs:hs + 512]),
            "wv_my": np.ascontiguousarray(np.asarray(inputs["L1_attn_Wv"], f)[:, hs:hs + 512]),
            "bq_my": np.ascontiguousarray(np.asarray(inputs["L1_attn_bq"], f)[hs:hs + 512]),
            "bk_my": np.ascontiguousarray(np.asarray(inputs["L1_attn_bk"], f)[hs:hs + 512]),
            "bv_my": np.ascontiguousarray(np.asarray(inputs["L1_attn_bv"], f)[hs:hs + 512]),
            "wo_my": np.ascontiguousarray(np.asarray(inputs["L1_attn_Wo"], f)[hs:hs + 512]),
            "bo": np.asarray(inputs["L1_attn_bo"], f),
            "w_ffnT": np.ascontiguousarray(np.asarray(inputs["L1_ffn_weight"], f).T),
            "flagv": np.ascontiguousarray(
                np.stack([np.full(128, 1.0 - pp, f), np.full(128, float(pp), f)], 1)),
            "ident_f": ident,
            "ident_b": identb_,
        }
        maps.append(m)
    return maps


def kernel(**inputs):
    maps = prepare_inputs(inputs)
    nc, tap_out = build(TAPS)
    res = run_bass_kernel_spmd(nc, maps, core_ids=list(range(8)))
    kernel.last = res
    out = np.empty((4, SEQ, DM), np.float32)
    for k in range(8):
        b, pp = k // 2, k % 2
        out[b, pp * TOK:(pp + 1) * TOK] = res.results[k]["out_my"]
    return out
